# version 01
# baseline (speedup 1.0000x reference)
import contextlib
import numpy as np
import ml_dtypes
import concourse.bass as bass
import concourse.mybir as mybir
from concourse.bass_utils import run_bass_kernel_spmd

F32 = mybir.dt.float32
BF16 = mybir.dt.bfloat16
I32 = mybir.dt.int32
I16 = mybir.dt.int16
AF = mybir.ActivationFunctionType
ALU = mybir.AluOpType
AXX = mybir.AxisListType.X

B, S, D = 4, 4096, 2048
Hi, Di, H, hd = 16, 128, 16, 128
TOPK = 256
INTER = 5504
NOC = INTER // 128
NPOS = 16
NKC = D // 128
NITER = 28
NEGC = -2e36
RT = float(hd) ** -0.5


def _own_tiles(half):
    return [g for g in range(32) if (g % 4) in ((0, 3) if half == 0 else (1, 2))]


def _pos_glist(half):
    gs = _own_tiles(half)
    cnt = lambda g: (g + 4) // 4
    bigs = sorted(gs, key=lambda g: (-cnt(g), g))[:8]
    smalls = sorted(gs, key=lambda g: (cnt(g), g))[:8]
    order = []
    for i in range(8):
        order.append(bigs[i]); order.append(smalls[i])
    return order


CNT = [(g + 4) // 4 for g in _pos_glist(0)]
assert CNT == [(g + 4) // 4 for g in _pos_glist(1)] and sum(CNT) == 72


class _Op:
    __slots__ = ("idx", "eng", "fn", "waits", "dma", "marked", "semkey", "count", "waitvals", "ndma")


class Res:
    __slots__ = ("w", "r")

    def __init__(self):
        self.w = None
        self.r = []


class Sched:
    def __init__(self):
        self.ops = []

    def add(self, eng, fn, reads=(), writes=(), waits=(), dma=False, ndma=1):
        o = _Op()
        o.idx = len(self.ops); o.eng = eng; o.fn = fn; o.dma = dma
        o.ndma = ndma
        o.marked = False
        o.semkey = ("q_" + eng) if dma else eng
        o.count = None; o.waitvals = None
        w = set()
        for r in reads:
            if r.w is not None:
                w.add(r.w)
        for res in writes:
            if res.w is not None:
                w.add(res.w)
            w.update(res.r)
        for x in waits:
            if x is not None:
                w.add(x.idx if isinstance(x, _Op) else int(x))
        o.waits = w
        self.ops.append(o)
        for r in reads:
            r.r.append(o.idx)
        for res in writes:
            res.w = o.idx
            res.r = []
        return o

    def finalize(self):
        ops = self.ops

        def _skip(p, o):
            return (p.eng == o.eng == "tensor") and not p.dma and not o.dma

        for o in ops:
            if o.dma:
                o.marked = True
            for wi in o.waits:
                p = ops[wi]
                if _skip(p, o):
                    continue
                p.marked = True
        counts = {}
        for o in ops:
            if o.marked:
                amt = (16 * o.ndma) if o.dma else 1
                counts[o.semkey] = counts.get(o.semkey, 0) + amt
                o.count = counts[o.semkey]
        for o in ops:
            wv = {}
            for wi in o.waits:
                p = ops[wi]
                if not p.marked or _skip(p, o):
                    continue
                wv[p.semkey] = max(wv.get(p.semkey, 0), p.count)
            o.waitvals = wv
        seen = {}
        for o in ops:
            m = seen.setdefault(o.eng, {})
            wv = {}
            for k, v in o.waitvals.items():
                if m.get(k, -1) < v:
                    wv[k] = v
                    m[k] = v
            o.waitvals = wv
        return counts


def _hadamard(n):
    h = np.array([[1.0]])
    while h.shape[0] < n:
        h = np.block([[h, h], [h, -h]])
    return h * n ** (-0.5)


def _bf(a):
    return np.ascontiguousarray(np.asarray(a).astype(ml_dtypes.bfloat16))


def _tile_lhsT(w):
    rows, cols = w.shape
    kc, mc = rows // 128, cols // 128
    a = w.reshape(kc, 128, mc, 128)
    return np.ascontiguousarray(a.transpose(2, 1, 0, 3).reshape(mc * 128, kc * 128))


def _host_core_inputs(core, inputs):
    b, half = core // 2, core % 2
    glist = _pos_glist(half)
    x = inputs["L0_attn_x"][b]
    xq = np.concatenate([x[g * 128:(g + 1) * 128] for g in glist], 0)
    rowc = np.zeros((NPOS * 128, 8), np.float32)
    caps = np.zeros((NPOS * 128, 512), np.float32)
    c01 = np.zeros((NPOS * 128, 512), np.float32)
    for p, g in enumerate(glist):
        t = g * 128 + np.arange(128)
        n = (t + 1).astype(np.float64)
        short = n <= TOPK
        with np.errstate(divide="ignore", invalid="ignore"):
            k1 = np.sqrt(TOPK / np.maximum(n - TOPK, 1e-9))
            k2 = np.sqrt(np.maximum(n - TOPK, 0.0) / TOPK)
        sl = p * 128
        rowc[sl:sl + 128, 0] = 1.0 / (2.0 * n)
        rowc[sl:sl + 128, 6] = 1.0 / (4.0 * n)
        rowc[sl:sl + 128, 1] = np.where(short, 0.0, 1.0)
        rowc[sl:sl + 128, 2] = np.where(short, 0.0, -(1.05 * k1 + 1e-3))
        rowc[sl:sl + 128, 3] = np.where(short, NEGC, 0.0)
        rowc[sl:sl + 128, 4] = np.where(short, 0.0, (1.05 * k2 + 1e-3))
        rowc[sl:sl + 128, 5] = np.where(short, NEGC, 0.0)
        sg = (CNT[p] - 1) * 512 + np.arange(512)[None, :]
        causal = (t[:, None] >= sg)
        caps[sl:sl + 128] = np.where(causal, 3e38, -2e38)
        c01[sl:sl + 128] = np.where(causal, 1.0, 0.0)

    def tile_gu(idx):
        a = idx.reshape(NOC, 128, NKC, 128)
        return np.ascontiguousarray(a.transpose(0, 3, 2, 1).reshape(NOC * 128, NKC * 128))

    return {
        "xq": np.ascontiguousarray(xq),
        "xb": np.ascontiguousarray(x),
        "wqi_t": _tile_lhsT(inputs["L0_attn_wq_idx"]),
        "wki_t": _tile_lhsT(inputs["L0_attn_wk_idx"]),
        "wq_t": _tile_lhsT(inputs["L0_attn_wq"]),
        "w_iw": inputs["L0_attn_w_iw"],
        "wk": inputs["L0_attn_wk"],
        "wv": inputs["L0_attn_wv"],
        "wo": inputs["L0_attn_wo"],
        "lutv": np.ascontiguousarray(inputs["L0_ffn_lut"].reshape(1, 16)),
        "idxg": tile_gu(inputs["L0_ffn_gate_idx"]),
        "idxu": tile_gu(inputs["L0_ffn_up_idx"]),
        "idxd": np.ascontiguousarray(inputs["L0_ffn_down_idx"].T),
        "rowc": rowc,
        "caps": _bf(caps),
        "c01": _bf(c01),
        "hs": _bf(_hadamard(Di)),
        "ident": _bf(np.eye(128)),
        "ones": _bf(np.ones((128, 128))),
    }


def build_program(debug=()):
    debug = set(debug)
    nc = bass.Bass("TRN2", target_bir_lowering=False, debug=False)
    sch = Sched()
    A = sch.add

    din = {}
    for name, shape, dt in [
        ("xq", (NPOS * 128, D), F32), ("xb", (S, D), F32),
        ("wqi_t", (D, D), F32), ("wki_t", (Di, D), F32), ("wq_t", (D, D), F32),
        ("w_iw", (D, Hi), F32), ("wk", (D, hd), F32), ("wv", (D, hd), F32),
        ("wo", (D, D), F32), ("lutv", (1, 16), F32),
        ("idxg", (NOC * 128, D), I32), ("idxu", (NOC * 128, D), I32),
        ("idxd", (INTER, D), I32),
        ("rowc", (NPOS * 128, 8), F32),
        ("caps", (NPOS * 128, 512), BF16), ("c01", (NPOS * 128, 512), BF16),
        ("hs", (128, 128), BF16), ("ident", (128, 128), BF16), ("ones", (128, 128), BF16),
    ]:
        din[name] = nc.dram_tensor(name, list(shape), dt, kind="ExternalInput").ap()
    y_out = nc.dram_tensor("y", [NPOS * 128, D], F32, kind="ExternalOutput").ap()
    q_scr = nc.dram_tensor("q_scr", [NPOS * 128, D], BF16).ap()
    x2_scr = nc.dram_tensor("x2_scr", [NPOS * 128, D], F32).ap()
    h_scr = nc.dram_tensor("h_scr", [INTER, D], BF16).ap()
    wo_scr = nc.dram_tensor("wo_scr", [D, D], BF16).ap()
    wd_scr = nc.dram_tensor("wd_scr", [INTER, D], BF16).ap()
    devnull = nc.dram_tensor("devnull", [128, 4096], F32).ap()
    dbg = {}
    for name, shape, dt in [
        ("dbg_kT", (128, S), BF16), ("dbg_kiT", (128, S), BF16),
        ("dbg_v", (128, 32 * 128), BF16), ("dbg_qiT", (128, 16 * 2048), BF16),
        ("dbg_iw", (128, 256), F32), ("dbg_I", (128, 4096), F32),
        ("dbg_stat", (128, 64), F32), ("dbg_lo2", (128, NPOS), F32),
        ("dbg_mask", (128, 4096), BF16), ("dbg_oT", (128, 16 * 2048), BF16),
        ("dbg_x2", (NPOS * 128, D), F32), ("dbg_q", (NPOS * 128, D), BF16),
        ("dbg_maskT", (128, 4096), BF16), ("dbg_r", (128, 2048), F32),
        ("dbg_est", (128, 512), BF16), ("dbg_pst", (128, 512), BF16),
        ("dbg_rcp", (128, 512), F32), ("dbg_oacc", (128, 512), F32),
        ("dbg_pst4", (128, 4 * 512), BF16),
        ("dbg_sc", (128, 512), F32),
        ("dbg_wg", (128, 2048), BF16),
    ]:
        if name in debug:
            dbg[name] = nc.dram_tensor(name, list(shape), dt, kind="ExternalOutput").ap()

    ctx = contextlib.ExitStack()
    sb = lambda name, shape, dt: ctx.enter_context(nc.sbuf_tensor(name, list(shape), dt))

    BUF_A = sb("BUF_A", (128, 16 * 2048), BF16)
    IBUF = [sb("IBUF0", (128, 4096), F32), sb("IBUF1", (128, 2048), F32)]
    JNK = sb("JNK", (128, 4096), BF16)
    kT = sb("kT", (128, S), BF16)
    kiT = sb("kiT", (128, S), BF16)
    v_sb = sb("v_sb", (128, 32 * 128), BF16)
    mask_ts = sb("mask_ts", (128, 4096), BF16)
    maskTT = sb("maskTT", (128, 8192), BF16)
    maskT = [maskTT[:, :4096], maskTT[:, 4096:]]
    qstage = [sb(f"qstage{i}", (128, 2048), BF16) for i in range(2)]
    qT_sb = [sb(f"qT{i}", (128, 2048), BF16) for i in range(2)]
    diag_sb = sb("diag_sb", (128, 16 * 128), BF16)
    Rst = [sb(f"Rst{i}", (128, 512), BF16) for i in range(2)]
    est = [sb(f"est{i}", (128, 512), BF16) for i in range(2)]
    pst = [sb(f"pst{i}", (128, 512), BF16) for i in range(2)]
    recip_t = sb("recip_t", (128, 1024), F32)
    recip = [recip_t[:, :512], recip_t[:, 512:]]
    i0scr = sb("i0scr", (128, 512), F32)
    flsh = sb("flsh", (128, 8), BF16)
    sqscr = sb("sqscr", (128, 512), F32)
    iw_sb = sb("iw_sb", (128, 256), F32)
    stat = sb("stat", (128, 64), F32)
    smal = sb("smal", (128, 24), F32)
    gei = sb("gei", (128, 2), mybir.dt.uint8)
    rowc_sb = [sb(f"rowc{i}", (128, 8), F32) for i in range(2)]
    caps_sb = [sb(f"caps{i}", (128, 512), BF16) for i in range(2)]
    c01_sb = [sb(f"c01_{i}", (128, 512), BF16) for i in range(2)]
    hs_sb = sb("hs_sb", (128, 128), BF16)
    id_sb = sb("id_sb", (128, 128), BF16)
    on_sb = sb("on_sb", (128, 128), BF16)
    lut_sb = sb("lut_sb", (128, 16), F32)
    wiw_sb = sb("wiw_sb", (128, 256), BF16)
    wbig = [sb(f"wbig{i}", (128, 2048), BF16) for i in range(3)]
    st5 = [sb(f"st5_{i}", (128, 512), BF16) for i in range(2)]

    xTg = [IBUF[0][:].bitcast(BF16), maskTT[:]]
    xst = [qstage[0][:], qstage[1][:]]
    wsmk = [mask_ts[:, :2048], mask_ts[:, 2048:]]
    wkit = qT_sb[0][:]
    qtok = [mask_ts[:, :2048], mask_ts[:, 2048:],
            diag_sb[:], recip_t[:].bitcast(BF16)]
    xres = [IBUF[0][:, :2048], IBUF[0][:, 2048:]]
    rtile = kT[:].bitcast(F32)
    x2bst = kiT[:, :2048]
    ib16b = IBUF[1][:].bitcast(I16)
    idx_d = [ib16b[:, :2048], ib16b[:, 2048:4096]]
    idx_g = [JNK[:].bitcast(I16)[:, :2048], JNK[:].bitcast(I16)[:, 2048:4096]]
    idx_u = [IBUF[0][:].bitcast(I16)[:, :2048], IBUF[0][:].bitcast(I16)[:, 2048:4096]]
    Wgb = [mask_ts[:, :2048], mask_ts[:, 2048:]]
    Wub = [maskT[0][:, :2048], maskT[0][:, 2048:]]
    Wdb = [maskT[1][:, :2048], maskT[1][:, 2048:]]
    mtmp = [v_sb[:, :2048], v_sb[:, 2048:]]
    hTst = [kiT[:, :2048], kiT[:, 2048:]]
    hrd = [qstage[0][:], qstage[1][:]]
    wdrd = [qT_sb[0][:], qT_sb[1][:]]

    PSB = [ctx.enter_context(nc.psum_tensor(f"PS{i}", [128, 512], F32)) for i in range(8)]
    BANK = [t[:] for t in PSB]
    PD_bf = PSB[7][:].bitcast(BF16)
    RB = [Res() for _ in range(8)]

    r_A = [Res() for _ in range(16 * NPOS)]
    rA = lambda c, p: r_A[c * NPOS + p]
    r_I = [Res(), Res()]
    r_junk = r_I[0]
    r_kT = Res(); r_kiT = Res(); r_v = Res()
    r_mts = Res()
    r_qs = [Res(), Res()]; r_qT = [Res(), Res()]
    r_diag = Res(); r_R = [Res(), Res()]; r_e = [Res(), Res()]; r_p = [Res(), Res()]
    r_rc = [Res(), Res()]; r_i0 = Res(); r_sq = Res()
    r_iw = Res(); r_stat = Res(); r_smal = Res()
    r_rowc = [Res(), Res()]; r_caps = [Res(), Res()]; r_c01 = [Res(), Res()]
    r_const = Res(); r_wsm = Res()
    r_wbig = [Res() for _ in range(4)]
    r_st5 = [Res(), Res()]
    r_idxg = [Res(), Res()]; r_idxu = [Res(), Res()]; r_idxd = [Res(), Res()]
    r_hT = [Res(), Res()]; r_x2r = [Res(), Res()]
    r_xTg = [r_I[0], Res()]
    r_maskTT = r_xTg[1]
    r_mT = [r_maskTT, r_maskTT]
    r_xst = r_qs
    r_mtmp = [r_v, r_v]
    r_junk2 = Res()
    rd_q = Res(); rd_x2 = Res(); rd_h = Res(); rd_wo = Res(); rd_wd = Res()

    A("sync", lambda e: e.dma_start(out=hs_sb[:], in_=din["hs"]), writes=[r_const], dma=True)
    A("sync", lambda e: e.dma_start(out=id_sb[:], in_=din["ident"]), writes=[r_const], dma=True)
    A("sync", lambda e: e.dma_start(out=on_sb[:], in_=din["ones"]), writes=[r_const], dma=True)
    A("sync", lambda e: e.dma_start(out=lut_sb[:], in_=din["lutv"].partition_broadcast(128)),
      writes=[r_const], dma=True)

    r_wiwt = Res()
    def ld_small(e):
        out = []
        for c in range(NKC):
            out.append(e.dma_start(out=mask_ts[:, c * 128:(c + 1) * 128],
                                   in_=din["wk"][c * 128:(c + 1) * 128, :]))
        for c in range(NKC):
            out.append(e.dma_start(out=mask_ts[:, 2048 + c * 128:2048 + (c + 1) * 128],
                                   in_=din["wv"][c * 128:(c + 1) * 128, :]))
        return out
    A("gpsimd", ld_small, writes=[r_mts], dma=True, ndma=2 * NKC)
    A("gpsimd", (lambda e: [e.dma_start(out=wiw_sb[:, c * 16:(c + 1) * 16],
                                        in_=din["w_iw"][c * 128:(c + 1) * 128, :])
                            for c in range(NKC)]),
      writes=[r_wiwt], dma=True, ndma=NKC)
    wk_t = lambda c: mask_ts[:, c * 128:(c + 1) * 128]
    wv_t = lambda c: mask_ts[:, 2048 + c * 128:2048 + (c + 1) * 128]
    wiw_t = lambda c: wiw_sb[:, c * 16:(c + 1) * 16]
    A("gpsimd", lambda e: e.dma_start(out=wkit, in_=din["wki_t"][0:128, :]),
      writes=[r_qT[0]], dma=True)
    wki_t = lambda c: wkit[:, c * 128:(c + 1) * 128]

    def stage_tokens(src, row0, gslot, tig, xslot):
        ld = A("gpsimd", (lambda e, src=src, row0=row0, xslot=xslot:
                          e.dma_start(out=xst[xslot][:], in_=src[row0:row0 + 128, :])),
               writes=[r_xst[xslot]], dma=True)
        for c0 in (0, 8):
            tr = A("tensor", (lambda t, c0=c0, xslot=xslot:
                              [t.transpose(PD_bf[:, (c - c0) * 128:(c - c0 + 1) * 128],
                                           xst[xslot][:, c * 128:(c + 1) * 128], id_sb[:])
                               for c in range(c0, c0 + 8)][-1]),
                   reads=[r_xst[xslot], r_const], writes=[RB[7]], waits=[ld])
            A("scalar", (lambda a, c0=c0, gslot=gslot, tig=tig:
                         [a.copy(xTg[gslot][:, c * 512 + tig * 128: c * 512 + tig * 128 + 128],
                                 PD_bf[:, (c - c0) * 128:(c - c0 + 1) * 128])
                          for c in range(c0, c0 + 8)][-1]),
              reads=[RB[7]], writes=[r_xTg[gslot]], waits=[tr])

    for grp in range(8):
        gs = grp % 2
        for ti in range(4):
            stage_tokens(din["xb"], grp * 512 + ti * 128, gs, ti, ti % 2)
        mm = A("tensor", (lambda t, gs=gs:
                          [t.matmul(BANK[0], wk_t(c), xTg[gs][:, c * 512:(c + 1) * 512],
                                    start=(c == 0), stop=(c == NKC - 1), skip_group_check=True)
                           for c in range(NKC)][-1]),
               reads=[r_xTg[gs], r_mts], writes=[RB[0]])
        A("scalar", (lambda a, grp=grp: a.copy(kT[:, grp * 512:(grp + 1) * 512], BANK[0])),
          reads=[RB[0]], writes=[r_kT])
        mm = A("tensor", (lambda t, gs=gs:
                          [t.matmul(BANK[1], wki_t(c), xTg[gs][:, c * 512:(c + 1) * 512],
                                    start=(c == 0), stop=(c == NKC - 1), skip_group_check=True)
                           for c in range(NKC)][-1]),
               reads=[r_xTg[gs], r_qT[0]], writes=[RB[1]])
        A("scalar", (lambda a, gs=gs: a.copy(st5[gs][:], BANK[1])),
          reads=[RB[1]], writes=[r_st5[gs]])
        A("tensor", (lambda t, gs=gs: t.matmul(BANK[1], hs_sb[:], st5[gs][:],
                                               start=True, stop=True, skip_group_check=True)),
          reads=[r_st5[gs], r_const], writes=[RB[1]])
        A("scalar", (lambda a, grp=grp: a.copy(kiT[:, grp * 512:(grp + 1) * 512], BANK[1])),
          reads=[RB[1]], writes=[r_kiT])
        for ti in range(4):
            mm = A("tensor", (lambda t, gs=gs, ti=ti:
                              [t.matmul(BANK[4][:, :128],
                                        xTg[gs][:, c * 512 + ti * 128: c * 512 + ti * 128 + 128],
                                        wv_t(c), start=(c == 0), stop=(c == NKC - 1),
                                        skip_group_check=True)
                               for c in range(NKC)][-1]),
                   reads=[r_xTg[gs], r_mts], writes=[RB[4]])
            A("scalar", (lambda a, grp=grp, ti=ti:
                         a.copy(v_sb[:, (grp * 4 + ti) * 128:(grp * 4 + ti + 1) * 128],
                                BANK[4][:, :128])),
              reads=[RB[4]], writes=[r_v])

    for c in range(NKC):
        s = c % 2
        ld = A("gpsimd", (lambda e, c=c, s=s:
                          e.dma_start(out=wbig[s][:], in_=din["wo"][c * 128:(c + 1) * 128, :])),
               writes=[r_wbig[s]], dma=True)
        A("sync", (lambda e, c=c, s=s:
                   e.dma_start(out=wo_scr[c * 128:(c + 1) * 128, :], in_=wbig[s][:])),
          reads=[r_wbig[s]], writes=[rd_wo], waits=[ld], dma=True)

    for grp in range(4):
        gs = grp % 2
        for ti in range(4):
            stage_tokens(din["xq"], grp * 512 + ti * 128, gs, ti, ti % 2)
        for ti in range(4):
            p = grp * 4 + ti
            mm = A("tensor", (lambda t, gs=gs, ti=ti:
                              [t.matmul(BANK[5][:, :16],
                                        xTg[gs][:, c * 512 + ti * 128: c * 512 + ti * 128 + 128],
                                        wiw_t(c), start=(c == 0), stop=(c == NKC - 1),
                                        skip_group_check=True)
                               for c in range(NKC)][-1]),
                   reads=[r_xTg[gs], r_wiwt], writes=[RB[5]])
            A("vector", (lambda ve, p=p: ve.tensor_copy(iw_sb[:, p * 16:(p + 1) * 16],
                                                        BANK[5][:, :16])),
              reads=[RB[5]], writes=[r_iw])
        for hc in range(16):
            s = hc % 2
            ldi = A("gpsimd", (lambda e, hc=hc, s=s:
                               e.dma_start(out=wbig[s][:], in_=din["wqi_t"][hc * 128:(hc + 1) * 128, :])),
                    writes=[r_wbig[s]], dma=True)
            A("tensor", (lambda t, gs=gs, s=s:
                         [t.matmul(BANK[0], wbig[s][:, c * 128:(c + 1) * 128],
                                   xTg[gs][:, c * 512:(c + 1) * 512],
                                   start=(c == 0), stop=(c == NKC - 1), skip_group_check=True)
                          for c in range(NKC)][-1]),
              reads=[r_xTg[gs], r_wbig[s]], writes=[RB[0]], waits=[ldi])
            A("scalar", (lambda a, s=s: a.copy(st5[s][:], BANK[0])),
              reads=[RB[0]], writes=[r_st5[s]])
            A("tensor", (lambda t, s=s: t.matmul(BANK[1], hs_sb[:], st5[s][:],
                                                 start=True, stop=True, skip_group_check=True)),
              reads=[r_st5[s], r_const], writes=[RB[1]])
            A("scalar", (lambda a, hc=hc, grp=grp:
                         [a.copy(BUF_A[:, hc * 2048 + (grp * 4 + ti) * 128:
                                 hc * 2048 + (grp * 4 + ti) * 128 + 128],
                                 BANK[1][:, ti * 128:(ti + 1) * 128])
                          for ti in range(4)][-1]),
              reads=[RB[1]], writes=[rA(hc, grp * 4 + ti2) for ti2 in range(4)])
            ldq = A("gpsimd", (lambda e, hc=hc:
                               e.dma_start(out=wbig[2][:], in_=din["wq_t"][hc * 128:(hc + 1) * 128, :])),
                    writes=[r_wbig[2]], dma=True)
            A("tensor", (lambda t, gs=gs:
                         [t.matmul(BANK[2], wbig[2][:, c * 128:(c + 1) * 128],
                                   xTg[gs][:, c * 512:(c + 1) * 512],
                                   start=(c == 0), stop=(c == NKC - 1), skip_group_check=True)
                          for c in range(NKC)][-1]),
              reads=[r_xTg[gs], r_wbig[2]], writes=[RB[2]], waits=[ldq])
            A("scalar", (lambda a, s=s: a.copy(st5[s][:], BANK[2])),
              reads=[RB[2]], writes=[r_st5[s]])
            A("tensor", (lambda t, s=s:
                         [t.transpose(PD_bf[:, i * 128:(i + 1) * 128],
                                      st5[s][:, i * 128:(i + 1) * 128], id_sb[:])
                          for i in range(4)][-1]),
              reads=[r_st5[s], r_const], writes=[RB[7]])
            A("scalar", (lambda a, hc=hc:
                         [a.copy(qtok[ti][:, hc * 128:(hc + 1) * 128],
                                 PD_bf[:, ti * 128:(ti + 1) * 128])
                          for ti in range(4)][-1]),
              reads=[RB[7]], writes=[r_mts, r_diag, r_rc[0], r_rc[1]])
        for ti in range(4):
            p = grp * 4 + ti
            A("sync", (lambda e, p=p, ti=ti:
                       e.dma_start(out=q_scr[p * 128:(p + 1) * 128, :], in_=qtok[ti][:])),
              reads=[r_mts, r_diag, r_rc[0], r_rc[1]], writes=[rd_q], dma=True)

    if "dbg_kT" in dbg:
        A("sync", lambda e: e.dma_start(out=dbg["dbg_kT"], in_=kT[:]), reads=[r_kT], dma=True)
    if "dbg_kiT" in dbg:
        A("sync", lambda e: e.dma_start(out=dbg["dbg_kiT"], in_=kiT[:]), reads=[r_kiT], dma=True)
    if "dbg_v" in dbg:
        A("sync", lambda e: e.dma_start(out=dbg["dbg_v"], in_=v_sb[:]), reads=[r_v], dma=True)
    if "dbg_qiT" in dbg:
        A("sync", lambda e: e.dma_start(out=dbg["dbg_qiT"], in_=BUF_A[:]),
          reads=[rA(c, p) for c in range(16) for p in range(NPOS)], dma=True)
    if "dbg_iw" in dbg:
        A("sync", lambda e: e.dma_start(out=dbg["dbg_iw"], in_=iw_sb[:]), reads=[r_iw], dma=True)
    if "dbg_q" in dbg:
        A("sync", lambda e: e.dma_start(out=dbg["dbg_q"], in_=q_scr), reads=[rd_q], dma=True)

    if debug and not (debug - {"dbg_kT", "dbg_kiT", "dbg_v", "dbg_qiT", "dbg_iw", "dbg_q"}):
        return _finish(nc, sch, ctx)

    SL_SI, SL_SQ, SL_MU, SL_MS, SL_VAR, SL_T1, SL_T2 = 0, 1, 2, 3, 4, 5, 6
    SL_LO = [7, 8]; SL_HI = [9, 10]; SL_MS2, SL_CNT, SL_GE = 11, 12, 13
    sm = lambda c: smal[:, c:c + 1]

    def emit_importance(p):
        ib = p % 2
        E = 512 * CNT[p]
        ec = CNT[p] - 1
        ibuf = IBUF[ib]
        A("sync", (lambda e, p=p, ib=ib:
                   e.dma_start(out=rowc_sb[ib][:], in_=din["rowc"][p * 128:(p + 1) * 128, :])),
          writes=[r_rowc[ib]], dma=True)
        A("sync", (lambda e, p=p, ib=ib:
                   e.dma_start(out=caps_sb[ib][:], in_=din["caps"][p * 128:(p + 1) * 128, :])),
          writes=[r_caps[ib]], dma=True)
        A("sync", (lambda e, p=p, ib=ib:
                   e.dma_start(out=c01_sb[ib][:], in_=din["c01"][p * 128:(p + 1) * 128, :])),
          writes=[r_c01[ib]], dma=True)
        A("vector", (lambda ve, p=p:
                     [ve.tensor_scalar(diag_sb[:, h * 128:(h + 1) * 128], id_sb[:],
                                       iw_sb[:, p * 16 + h:p * 16 + h + 1], None, ALU.mult)
                      for h in range(16)][-1]),
          reads=[r_iw, r_const], writes=[r_diag])
        for c in range(CNT[p]):
            for h in range(16):
                pg = h % 2
                A("tensor", (lambda t, h=h, p=p, c=c, pg=pg:
                             t.matmul(BANK[pg], BUF_A[:, h * 2048 + p * 128: h * 2048 + p * 128 + 128],
                                      kiT[:, c * 512:(c + 1) * 512],
                                      start=True, stop=True, skip_group_check=True)),
                  reads=[rA(h, p), r_kiT], writes=[RB[pg]])
                A("scalar", (lambda a, pg=pg: a.activation(Rst[pg][:], BANK[pg], AF.Relu)),
                  reads=[RB[pg]], writes=[r_R[pg]])
                A("tensor", (lambda t, h=h, pg=pg:
                             t.matmul(BANK[2], diag_sb[:, h * 128:(h + 1) * 128], Rst[pg][:],
                                      start=(h == 0), stop=(h == 15), skip_group_check=True)),
                  reads=[r_R[pg], r_diag],
                  writes=[RB[2]] if h in (0, 15) else [])
            if c == ec:
                A("vector", (lambda ve, c=c, ib=ib:
                             ve.scalar_tensor_tensor(i0scr[:], BANK[2], 2.0, c01_sb[ib][:],
                                                     ALU.mult, ALU.mult,
                                                     accum_out=stat[:, c:c + 1])),
                  reads=[RB[2], r_c01[ib]], writes=[r_i0, r_stat])
                A("scalar", (lambda a, c=c: a.activation(sqscr[:], i0scr[:], AF.Square,
                                                         accum_out=stat[:, 8 + c:9 + c])),
                  reads=[r_i0], writes=[r_sq, r_stat])
                A("vector", (lambda ve, c=c, ib=ib, ibuf=ibuf:
                             ve.scalar_tensor_tensor(ibuf[:, c * 512:(c + 1) * 512], BANK[2],
                                                     2.0, caps_sb[ib][:], ALU.mult, ALU.min)),
                  reads=[RB[2], r_caps[ib]], writes=[r_I[ib]])
            else:
                A("vector", (lambda ve, c=c, ibuf=ibuf:
                             ve.tensor_scalar(ibuf[:, c * 512:(c + 1) * 512], BANK[2],
                                              2.0, None, ALU.mult, ALU.add,
                                              accum_out=stat[:, c:c + 1])),
                  reads=[RB[2]], writes=[r_I[ib], r_stat])
                A("scalar", (lambda a, c=c, ibuf=ibuf:
                             a.activation(sqscr[:], ibuf[:, c * 512:(c + 1) * 512], AF.Square,
                                          accum_out=stat[:, 8 + c:9 + c])),
                  reads=[r_I[ib]], writes=[r_sq, r_stat])
        n = CNT[p]
        A("vector", (lambda ve, n=n: ve.reduce_sum(sm(SL_SI), stat[:, 0:n], axis=AXX)),
          reads=[r_stat], writes=[r_smal])
        A("vector", (lambda ve, n=n: ve.reduce_sum(sm(SL_SQ), stat[:, 8:8 + n], axis=AXX)),
          reads=[r_stat], writes=[r_smal])
        A("vector", (lambda ve, ib=ib:
                     ve.tensor_scalar(sm(SL_MU), sm(SL_SI), rowc_sb[ib][:, 0:1], None, ALU.mult)),
          reads=[r_rowc[ib]], writes=[r_smal])
        A("vector", (lambda ve, ib=ib:
                     ve.tensor_scalar(sm(SL_MS), sm(SL_SQ), rowc_sb[ib][:, 6:7], None, ALU.mult)),
          reads=[r_rowc[ib]], writes=[r_smal])
        A("vector", lambda ve: ve.tensor_tensor(sm(SL_T1), sm(SL_MU), sm(SL_MU), ALU.mult),
          writes=[r_smal])
        A("vector", lambda ve: ve.tensor_tensor(sm(SL_VAR), sm(SL_MS), sm(SL_T1), ALU.subtract),
          writes=[r_smal])
        A("vector", lambda ve: ve.tensor_scalar(sm(SL_VAR), sm(SL_VAR), 1e-20, None, ALU.max),
          writes=[r_smal])
        A("scalar", lambda a: a.activation(sm(SL_T1), sm(SL_VAR), AF.Ln),
          reads=[r_smal], writes=[r_smal])
        A("scalar", lambda a: a.activation(sm(SL_T2), sm(SL_T1), AF.Exp, scale=0.5),
          writes=[r_smal])
        A("vector", (lambda ve, ib=ib:
                     ve.tensor_scalar(sm(SL_T1), sm(SL_MU), rowc_sb[ib][:, 1:2], None, ALU.mult)),
          writes=[r_smal])
        A("vector", (lambda ve, ib=ib:
                     ve.scalar_tensor_tensor(sm(SL_LO[0]), sm(SL_T2), rowc_sb[ib][:, 2:3],
                                             sm(SL_T1), ALU.mult, ALU.add)),
          writes=[r_smal])
        A("vector", (lambda ve, ib=ib:
                     ve.tensor_scalar(sm(SL_LO[0]), sm(SL_LO[0]), rowc_sb[ib][:, 3:4], None, ALU.add)),
          writes=[r_smal])
        A("vector", (lambda ve, ib=ib:
                     ve.scalar_tensor_tensor(sm(SL_HI[0]), sm(SL_T2), rowc_sb[ib][:, 4:5],
                                             sm(SL_T1), ALU.mult, ALU.add)),
          writes=[r_smal])
        A("vector", (lambda ve, ib=ib:
                     ve.tensor_scalar(sm(SL_HI[0]), sm(SL_HI[0]), rowc_sb[ib][:, 5:6], None, ALU.add)),
          writes=[r_smal])
        for it in range(NITER):
            pa, pb = it % 2, 1 - it % 2
            A("vector", (lambda ve, pa=pa: ve.tensor_tensor(sm(SL_MS2), sm(SL_LO[pa]),
                                                            sm(SL_HI[pa]), ALU.add)),
              writes=[r_smal])
            A("vector", (lambda ve, E=E, ibuf=ibuf:
                         ve.tensor_scalar(JNK[:, :E], ibuf[:, :E], sm(SL_MS2), None,
                                          ALU.is_ge, ALU.add, accum_out=sm(SL_CNT))),
              reads=[r_I[ib]], writes=[r_junk2, r_smal])
            A("vector", lambda ve: ve.tensor_scalar(gei[:, 0:1], sm(SL_CNT), float(TOPK), None,
                                                    ALU.is_ge),
              writes=[r_smal])
            A("vector", lambda ve: ve.tensor_scalar(sm(SL_MS2), sm(SL_MS2), 0.5, None, ALU.mult),
              writes=[r_smal])
            A("vector", (lambda ve, pa=pa, pb=pb:
                         ve.select(sm(SL_LO[pb]), gei[:, 0:1], sm(SL_MS2), sm(SL_LO[pa]))),
              writes=[r_smal])
            A("vector", (lambda ve, pa=pa, pb=pb:
                         ve.select(sm(SL_HI[pb]), gei[:, 0:1], sm(SL_HI[pa]), sm(SL_MS2))),
              writes=[r_smal])
        fin = NITER % 2
        A("vector", (lambda ve, fin=fin:
                     ve.tensor_tensor(sm(SL_MS2), sm(SL_LO[fin]), sm(SL_LO[fin]), ALU.add)),
          writes=[r_smal])
        if "dbg_lo2" in dbg:
            A("vector", (lambda ve, p=p, fin=fin:
                         ve.tensor_copy(stat[:, 32 + p:33 + p], sm(SL_LO[fin]))),
              writes=[r_stat])
        mko = A("vector", (lambda ve, E=E, ibuf=ibuf:
                           ve.tensor_scalar(mask_ts[:, :E], ibuf[:, :E], sm(SL_MS2), None,
                                            ALU.is_ge)),
                reads=[r_I[ib], r_smal], writes=[r_mts])
        if "dbg_I" in dbg and p == 1:
            A("sync", (lambda e, E=E, ibuf=ibuf: e.dma_start(out=dbg["dbg_I"][:, :E],
                                                             in_=ibuf[:, :E])),
              reads=[r_I[ib]], dma=True)
        if "dbg_mask" in dbg and p == 1:
            A("sync", (lambda e, E=E: e.dma_start(out=dbg["dbg_mask"][:, :E], in_=mask_ts[:, :E])),
              reads=[r_mts], waits=[mko], dma=True)
        if "dbg_stat" in dbg and p == 1:
            A("sync", lambda e: e.dma_start(out=dbg["dbg_stat"], in_=stat[:]),
              reads=[r_stat], dma=True)
        A("sync", (lambda e, E=E, ibuf=ibuf: e.dma_start(out=devnull[:, :E], in_=ibuf[:, :E])),
          reads=[r_I[ib]], dma=True)
        A("sync", (lambda e, E=E: e.dma_start(out=devnull[:].bitcast(BF16)[:, :E],
                                              in_=mask_ts[:, :E])),
          reads=[r_mts], waits=[mko], dma=True)
        mslot = p % 2
        for s0 in range(0, E // 128, 8):
            n8 = min(8, E // 128 - s0)
            A("tensor", (lambda t, s0=s0, n8=n8:
                         [t.transpose(PD_bf[:, i * 128:(i + 1) * 128],
                                      mask_ts[:, (s0 + i) * 128:(s0 + i + 1) * 128], id_sb[:])
                          for i in range(n8)][-1]),
              reads=[r_mts, r_const], writes=[RB[7]])
            A("scalar", (lambda a, s0=s0, n8=n8, mslot=mslot:
                         [a.copy(maskT[mslot][:, (s0 + i) * 128:(s0 + i + 1) * 128],
                                 PD_bf[:, i * 128:(i + 1) * 128])
                          for i in range(n8)][-1]),
              reads=[RB[7]], writes=[r_mT[mslot]])
        if "dbg_maskT" in dbg and p == 1:
            A("sync", (lambda e: e.dma_start(out=dbg["dbg_maskT"], in_=maskT[1][:, :4096])),
              reads=[r_mT[1]], dma=True)
        lq = A("sync", (lambda e, p=p, mslot=mslot:
                        e.dma_start(out=qstage[mslot][:], in_=q_scr[p * 128:(p + 1) * 128, :])),
               reads=[rd_q], writes=[r_qs[mslot]], dma=True)
        for c0 in (0, 8):
            A("tensor", (lambda t, c0=c0, mslot=mslot:
                         [t.transpose(PD_bf[:, (c - c0) * 128:(c - c0 + 1) * 128],
                                      qstage[mslot][:, c * 128:(c + 1) * 128], id_sb[:])
                          for c in range(c0, c0 + 8)][-1]),
              reads=[r_qs[mslot], r_const], writes=[RB[7]], waits=[lq])
            A("scalar", (lambda a, c0=c0, mslot=mslot:
                         [a.copy(qT_sb[mslot][:, c * 128:(c + 1) * 128],
                                 PD_bf[:, (c - c0) * 128:(c - c0 + 1) * 128])
                          for c in range(c0, c0 + 8)][-1]),
              reads=[RB[7]], writes=[r_qT[mslot]])

    def emit_attention(p):
        mslot = p % 2
        NS = 4 * CNT[p]
        for qtr in range(4):
            for sbk in range(NS):
                pg = sbk % 2
                A("tensor", (lambda t, sbk=sbk, qtr=qtr, pg=pg, mslot=mslot:
                             t.matmul(BANK[3 + pg], kT[:, sbk * 128:(sbk + 1) * 128],
                                      qT_sb[mslot][:, qtr * 512:(qtr + 1) * 512],
                                      start=True, stop=True, skip_group_check=True)),
                  reads=[r_kT, r_qT[mslot]], writes=[RB[3 + pg]])
                A("scalar", (lambda a, pg=pg: a.activation(est[pg][:], BANK[3 + pg],
                                                           AF.Exp, scale=RT)),
                  reads=[RB[3 + pg]], writes=[r_e[pg]])
                A("gpsimd", (lambda g, pg=pg, sbk=sbk, mslot=mslot:
                             g.tensor_tensor(
                                 pst[pg][:].rearrange("p (h t) -> p h t", h=4),
                                 est[pg][:].rearrange("p (h t) -> p h t", h=4),
                                 maskT[mslot][:, sbk * 128:(sbk + 1) * 128]
                                 .rearrange("p (o t) -> p o t", o=1)
                                 .to_broadcast([128, 4, 128]),
                                 ALU.mult)),
                  reads=[r_e[pg], r_mT[mslot]], writes=[r_p[pg]])
                if "dbg_est" in dbg and p == 1 and qtr == 2 and sbk == 0:
                    A("sync", (lambda e, pg=pg: e.dma_start(out=dbg["dbg_est"], in_=est[pg][:])),
                      reads=[r_e[pg]], dma=True)
                    A("sync", (lambda e, pg=pg: e.dma_start(out=dbg["dbg_pst"], in_=pst[pg][:])),
                      reads=[r_p[pg]], dma=True)
                if "dbg_pst4" in dbg and p == 1 and qtr == 2:
                    A("sync", (lambda e, pg=pg, sbk=sbk:
                               e.dma_start(out=dbg["dbg_pst4"][:, sbk * 512:(sbk + 1) * 512],
                                           in_=pst[pg][:])),
                      reads=[r_p[pg]], dma=True)
                A("tensor", (lambda t, sbk=sbk, pg=pg, first=(sbk == 0), lastc=(sbk == NS - 1):
                             t.matmul(BANK[5], v_sb[:, sbk * 128:(sbk + 1) * 128], pst[pg][:],
                                      start=first, stop=lastc, skip_group_check=True)),
                  reads=[r_p[pg], r_v],
                  writes=[RB[5]] if sbk in (0, NS - 1) else [])
                A("tensor", (lambda t, pg=pg, first=(sbk == 0), lastc=(sbk == NS - 1):
                             t.matmul(BANK[6], on_sb[:], pst[pg][:],
                                      start=first, stop=lastc, skip_group_check=True)),
                  reads=[r_p[pg], r_const],
                  writes=[RB[6]] if sbk in (0, NS - 1) else [])
            if "dbg_oacc" in dbg and p == 1 and qtr == 2:
                oev = A("scalar", (lambda a: a.copy(sqscr[:], BANK[5])), reads=[RB[5]],
                        writes=[r_sq])
                A("sync", (lambda e: e.dma_start(out=dbg["dbg_oacc"], in_=sqscr[:])),
                  reads=[r_sq], waits=[oev], dma=True)
            A("vector", (lambda ve, qtr=qtr: ve.reciprocal(recip[qtr % 2][:], BANK[6])),
              reads=[RB[6]], writes=[r_rc[qtr % 2]])
            if "dbg_rcp" in dbg and p == 1 and qtr == 2:
                A("sync", (lambda e, qtr=qtr: e.dma_start(out=dbg["dbg_rcp"], in_=recip[qtr % 2][:])),
                  reads=[r_rc[qtr % 2]], dma=True)
            A("vector", (lambda ve, p=p, qtr=qtr:
                         ve.tensor_tensor(
                             BUF_A[:].rearrange("p (h t) -> p h t", h=16)
                             [:, qtr * 4:(qtr + 1) * 4, p * 128:(p + 1) * 128],
                             BANK[5].rearrange("p (h t) -> p h t", h=4),
                             recip[qtr % 2][:].rearrange("p (h t) -> p h t", h=4),
                             ALU.mult)),
              reads=[RB[5], r_rc[qtr % 2]],
              writes=[rA(qtr * 4 + i, p) for i in range(4)])

    for p in range(NPOS):
        emit_importance(p)
        if p >= 1:
            emit_attention(p - 1)
    emit_attention(NPOS - 1)

    if "dbg_oT" in dbg:
        A("sync", lambda e: e.dma_start(out=dbg["dbg_oT"], in_=BUF_A[:]),
          reads=[rA(c, p) for c in range(16) for p in range(NPOS)], dma=True)
    if "dbg_lo2" in dbg:
        A("sync", lambda e: e.dma_start(out=dbg["dbg_lo2"], in_=stat[:, 32:48]),
          reads=[r_stat], dma=True)
    if debug and not (debug - {"dbg_I", "dbg_mask", "dbg_lo2", "dbg_stat", "dbg_oT", "dbg_maskT"}):
        return _finish(nc, sch, ctx)

    for p in range(NPOS):
        s = p % 2
        lx = A("sync", (lambda e, p=p, s=s:
                        e.dma_start(out=xres[s], in_=din["xq"][p * 128:(p + 1) * 128, :])),
               writes=[r_junk], dma=True)
        mmlast = None
        for c in range(NKC):
            ws = c % 2
            ld = A("gpsimd", (lambda e, c=c, ws=ws:
                              e.dma_start(out=wbig[ws][:], in_=wo_scr[c * 128:(c + 1) * 128, :])),
                   reads=[rd_wo], writes=[r_wbig[ws]], dma=True)
            mmlast = A("tensor", (lambda t, c=c, p=p, ws=ws, first=(c == 0), lastc=(c == NKC - 1):
                                  [t.matmul(BANK[d], BUF_A[:, c * 2048 + p * 128: c * 2048 + p * 128 + 128],
                                            wbig[ws][:, d * 512:(d + 1) * 512],
                                            start=first, stop=lastc, skip_group_check=True)
                                   for d in range(4)][-1]),
                       reads=[rA(c, p), r_wbig[ws]],
                       writes=[RB[0], RB[1], RB[2], RB[3]] if c in (0, NKC - 1) else [],
                       waits=[ld])
        for d in range(4):
            A("vector", (lambda ve, d=d, s=s:
                         ve.scalar_tensor_tensor(rtile[:, d * 512:(d + 1) * 512], BANK[d], 0.0,
                                                 xres[s][:, d * 512:(d + 1) * 512],
                                                 ALU.bypass, ALU.add,
                                                 accum_out=stat[:, 16 + d:17 + d])),
              reads=[RB[d], r_junk], writes=[r_kT, r_stat], waits=[lx, mmlast])
            A("scalar", (lambda a, d=d:
                         a.activation(sqscr[:], rtile[:, d * 512:(d + 1) * 512], AF.Square,
                                      accum_out=stat[:, 20 + d:21 + d])),
              reads=[r_kT], writes=[r_sq, r_stat])
        A("vector", lambda ve: ve.reduce_sum(sm(SL_SI), stat[:, 16:20], axis=AXX),
          reads=[r_stat], writes=[r_smal])
        A("vector", lambda ve: ve.reduce_sum(sm(SL_SQ), stat[:, 20:24], axis=AXX),
          reads=[r_stat], writes=[r_smal])
        A("vector", lambda ve: ve.tensor_scalar(sm(SL_MU), sm(SL_SI), 1.0 / D, None, ALU.mult),
          writes=[r_smal])
        A("vector", lambda ve: ve.tensor_scalar(sm(SL_MS), sm(SL_SQ), 1.0 / D, None, ALU.mult),
          writes=[r_smal])
        A("vector", lambda ve: ve.tensor_tensor(sm(SL_T1), sm(SL_MU), sm(SL_MU), ALU.mult),
          writes=[r_smal])
        A("vector", lambda ve: ve.tensor_tensor(sm(SL_VAR), sm(SL_MS), sm(SL_T1), ALU.subtract),
          writes=[r_smal])
        A("vector", lambda ve: ve.tensor_scalar(sm(SL_VAR), sm(SL_VAR), 1e-5, None, ALU.add),
          writes=[r_smal])
        A("scalar", lambda a: a.activation(sm(SL_T1), sm(SL_VAR), AF.Ln),
          reads=[r_smal], writes=[r_smal])
        A("scalar", lambda a: a.activation(sm(SL_T2), sm(SL_T1), AF.Exp, scale=-0.5),
          writes=[r_smal])
        if "dbg_r" in dbg and p == 1:
            A("sync", (lambda e: e.dma_start(out=dbg["dbg_r"], in_=rtile[:])),
              reads=[r_kT], dma=True)
        x2o = A("vector", (lambda ve, s=s:
                           ve.tensor_scalar(xres[s], rtile[:], sm(SL_MU), sm(SL_T2),
                                            ALU.subtract, ALU.mult)),
                reads=[r_kT, r_smal], writes=[r_junk])
        A("sync", (lambda e, p=p, s=s:
                   e.dma_start(out=x2_scr[p * 128:(p + 1) * 128, :], in_=xres[s])),
          reads=[r_junk], writes=[rd_x2], waits=[x2o], dma=True)
        cb = A("scalar", (lambda a, s=s: a.copy(x2bst[:], xres[s])),
               reads=[r_junk], writes=[r_kiT])
        for c0 in (0, 8):
            A("tensor", (lambda t, c0=c0:
                         [t.transpose(PD_bf[:, (c - c0) * 128:(c - c0 + 1) * 128],
                                      x2bst[:, c * 128:(c + 1) * 128], id_sb[:])
                          for c in range(c0, c0 + 8)][-1]),
              reads=[r_kiT, r_const], writes=[RB[7]], waits=[cb])
            A("scalar", (lambda a, c0=c0, p=p:
                         [a.copy(BUF_A[:, (c0 + i) * 2048 + p * 128: (c0 + i) * 2048 + p * 128 + 128],
                                 PD_bf[:, i * 128:(i + 1) * 128])
                          for i in range(8)][-1]),
              reads=[RB[7]], writes=[rA(c0 + i, p) for i in range(8)])

    if "dbg_x2" in dbg:
        A("sync", lambda e: e.dma_start(out=dbg["dbg_x2"], in_=x2_scr), reads=[rd_x2], dma=True)
    if debug and "dbg_x2" in dbg and "dbg_wg" not in dbg:
        return _finish(nc, sch, ctx)

    lut_s = lambda j: lut_sb[:, j:j + 1]

    def materialize(acc_ap, idx_ap, tmp_ap, r_acc, r_idx, r_tmp):
        A("vector", (lambda ve: ve.tensor_scalar(acc_ap, idx_ap, 0.0, lut_s(0),
                                                 ALU.is_equal, ALU.mult)),
          reads=[r_idx, r_const], writes=[r_acc])
        for j in range(1, 16):
            A("vector", (lambda ve, j=j: ve.tensor_scalar(tmp_ap, idx_ap, float(j), lut_s(j),
                                                          ALU.is_equal, ALU.mult)),
              reads=[r_idx, r_const], writes=[r_tmp])
            A("vector", (lambda ve: ve.tensor_tensor(acc_ap, acc_ap, tmp_ap, ALU.add)),
              reads=[r_tmp], writes=[r_acc])

    x2T_res = [rA(kc, p) for kc in range(16) for p in range(NPOS)]
    for oc in range(NOC):
        s = oc % 2
        A("gpsimd", (lambda e, oc=oc, s=s:
                     e.dma_start(out=idx_g[s], in_=din["idxg"][oc * 128:(oc + 1) * 128, :])),
          writes=[r_idxg[s], r_junk2], dma=True)
        A("gpsimd", (lambda e, oc=oc, s=s:
                     e.dma_start(out=idx_u[s], in_=din["idxu"][oc * 128:(oc + 1) * 128, :])),
          writes=[r_idxu[s], r_I[0]], dma=True)
        materialize(Wgb[s], idx_g[s], mtmp[0][:], r_mts, r_idxg[s], r_mtmp[0])
        materialize(Wub[s], idx_u[s], mtmp[1][:], r_mT[0], r_idxu[s], r_mtmp[1])
        for th in range(2):
            pp = (oc * 2 + th) % 2
            gpsu, upsu = (0, 1) if pp == 0 else (2, 3)
            mmg = A("tensor", (lambda t, s=s, th=th, gpsu=gpsu:
                               [t.matmul(BANK[2 * gpsu + half],
                                         Wgb[s][:, kc * 128:(kc + 1) * 128],
                                         BUF_A[:, kc * 2048 + th * 1024 + half * 512:
                                               kc * 2048 + th * 1024 + half * 512 + 512],
                                         start=(kc == 0), stop=(kc == NKC - 1),
                                         skip_group_check=True)
                                for half in range(2) for kc in range(NKC)][-1]),
                    reads=[r_mts] + x2T_res,
                    writes=[RB[2 * gpsu], RB[2 * gpsu + 1]])
            mmu = A("tensor", (lambda t, s=s, th=th, upsu=upsu:
                               [t.matmul(BANK[2 * upsu + half],
                                         Wub[s][:, kc * 128:(kc + 1) * 128],
                                         BUF_A[:, kc * 2048 + th * 1024 + half * 512:
                                               kc * 2048 + th * 1024 + half * 512 + 512],
                                         start=(kc == 0), stop=(kc == NKC - 1),
                                         skip_group_check=True)
                                for half in range(2) for kc in range(NKC)][-1]),
                    reads=[r_mT[0]] + x2T_res,
                    writes=[RB[2 * upsu], RB[2 * upsu + 1]])
            sl = A("scalar", (lambda a, gpsu=gpsu:
                              [a.activation(sqscr[:].bitcast(BF16)[:, half * 512:(half + 1) * 512],
                                            BANK[2 * gpsu + half], AF.Silu)
                               for half in range(2)][-1]),
                   reads=[RB[2 * gpsu], RB[2 * gpsu + 1]], writes=[r_sq], waits=[mmg])
            A("vector", (lambda ve, upsu=upsu, s=s, th=th:
                         [ve.tensor_tensor(hTst[s][:, th * 1024 + half * 512:th * 1024 + half * 512 + 512],
                                           sqscr[:].bitcast(BF16)[:, half * 512:(half + 1) * 512],
                                           BANK[2 * upsu + half], ALU.mult)
                          for half in range(2)][-1]),
              reads=[r_sq, RB[2 * upsu], RB[2 * upsu + 1]], writes=[r_hT[s]],
              waits=[mmu, sl])
        A("sync", (lambda e, oc=oc, s=s:
                   e.dma_start(out=h_scr[oc * 128:(oc + 1) * 128, :], in_=hTst[s][:])),
          reads=[r_hT[s]], writes=[rd_h], dma=True)
        if "dbg_wg" in dbg and oc == 0:
            A("sync", (lambda e, s=s: e.dma_start(out=dbg["dbg_wg"], in_=Wgb[s])),
              reads=[r_mts], dma=True)
    for ic in range(NOC):
        s = ic % 2
        A("gpsimd", (lambda e, ic=ic, s=s:
                     e.dma_start(out=idx_d[s], in_=din["idxd"][ic * 128:(ic + 1) * 128, :])),
          writes=[r_idxd[s], r_I[1]], dma=True)
        materialize(Wdb[s], idx_d[s], mtmp[s][:], r_mT[1], r_idxd[s], r_mtmp[s])
        A("sync", (lambda e, ic=ic, s=s:
                   e.dma_start(out=wd_scr[ic * 128:(ic + 1) * 128, :], in_=Wdb[s][:])),
          reads=[r_mT[1]], writes=[rd_wd], dma=True)

    if debug:
        return _finish(nc, sch, ctx)

    y_ops = []
    for pp in range(0, NPOS, 2):
        lx = []
        for j in range(2):
            lx.append(A("sync", (lambda e, pp=pp, j=j:
                                 e.dma_start(out=xres[j],
                                             in_=x2_scr[(pp + j) * 128:(pp + j + 1) * 128, :])),
                        reads=[rd_x2], writes=[r_x2r[j]], dma=True))
        mmlast = None
        for ic in range(NOC):
            s = ic % 2
            lh = A("sync", (lambda e, ic=ic, s=s:
                            e.dma_start(out=hrd[s], in_=h_scr[ic * 128:(ic + 1) * 128, :])),
                   reads=[rd_h], writes=[r_qs[s]], dma=True)
            lw = A("gpsimd", (lambda e, ic=ic, s=s:
                              e.dma_start(out=wdrd[s], in_=wd_scr[ic * 128:(ic + 1) * 128, :])),
                   reads=[rd_wd], writes=[r_qT[s]], dma=True)
            mmlast = A("tensor", (lambda t, ic=ic, s=s, pp=pp,
                                  first=(ic == 0), lastc=(ic == NOC - 1):
                                  [t.matmul(BANK[j * 4 + d],
                                            hrd[s][:, (pp + j) * 128:(pp + j + 1) * 128],
                                            wdrd[s][:, d * 512:(d + 1) * 512],
                                            start=first, stop=lastc, skip_group_check=True)
                                   for j in range(2) for d in range(4)][-1]),
                       reads=[r_qs[s], r_qT[s]],
                       writes=[RB[i] for i in range(8)] if ic in (0, NOC - 1) else [],
                       waits=[lh, lw])
        for j in range(2):
            p = pp + j
            for d in range(4):
                A("vector", (lambda ve, j=j, d=d:
                             ve.scalar_tensor_tensor(rtile[:, d * 512:(d + 1) * 512],
                                                     BANK[j * 4 + d], 0.0,
                                                     xres[j][:, d * 512:(d + 1) * 512],
                                                     ALU.bypass, ALU.add,
                                                     accum_out=stat[:, 16 + d:17 + d])),
                  reads=[RB[j * 4 + d], r_x2r[j]], writes=[r_kT, r_stat],
                  waits=[mmlast, lx[j]])
                A("scalar", (lambda a, d=d:
                             a.activation(sqscr[:], rtile[:, d * 512:(d + 1) * 512], AF.Square,
                                          accum_out=stat[:, 20 + d:21 + d])),
                  reads=[r_kT], writes=[r_sq, r_stat])
            A("vector", lambda ve: ve.reduce_sum(sm(SL_SI), stat[:, 16:20], axis=AXX),
              reads=[r_stat], writes=[r_smal])
            A("vector", lambda ve: ve.reduce_sum(sm(SL_SQ), stat[:, 20:24], axis=AXX),
              reads=[r_stat], writes=[r_smal])
            A("vector", lambda ve: ve.tensor_scalar(sm(SL_MU), sm(SL_SI), 1.0 / D, None, ALU.mult),
              writes=[r_smal])
            A("vector", lambda ve: ve.tensor_scalar(sm(SL_MS), sm(SL_SQ), 1.0 / D, None, ALU.mult),
              writes=[r_smal])
            A("vector", lambda ve: ve.tensor_tensor(sm(SL_T1), sm(SL_MU), sm(SL_MU), ALU.mult),
              writes=[r_smal])
            A("vector", lambda ve: ve.tensor_tensor(sm(SL_VAR), sm(SL_MS), sm(SL_T1), ALU.subtract),
              writes=[r_smal])
            A("vector", lambda ve: ve.tensor_scalar(sm(SL_VAR), sm(SL_VAR), 1e-5, None, ALU.add),
              writes=[r_smal])
            A("scalar", lambda a: a.activation(sm(SL_T1), sm(SL_VAR), AF.Ln),
              reads=[r_smal], writes=[r_smal])
            A("scalar", lambda a: a.activation(sm(SL_T2), sm(SL_T1), AF.Exp, scale=-0.5),
              writes=[r_smal])
            yo = A("vector", (lambda ve, j=j:
                              ve.tensor_scalar(xres[j], rtile[:], sm(SL_MU), sm(SL_T2),
                                               ALU.subtract, ALU.mult)),
                   reads=[r_kT, r_smal], writes=[r_x2r[j]])
            y_ops.append(A("sync", (lambda e, p=p, j=j:
                                    e.dma_start(out=y_out[p * 128:(p + 1) * 128, :], in_=xres[j])),
                           reads=[r_x2r[j]], waits=[yo], dma=True))
    A("sync", lambda e: e.nop(), waits=y_ops)
    return _finish(nc, sch, ctx)


def _finish(nc, sch, ctx):
    counts = sch.finalize()
    es = contextlib.ExitStack()
    sems = {k: es.enter_context(nc.semaphore(k)) for k in counts}
    per_eng = {}
    for o in sch.ops:
        per_eng.setdefault(o.eng, []).append(o)

    def runner(ops):
        def run(eng_obj):
            for o in ops:
                for k in sorted(o.waitvals):
                    eng_obj.wait_ge(sems[k], o.waitvals[k])
                inst = o.fn(eng_obj)
                if o.dma:
                    insts = inst if isinstance(inst, list) else [inst]
                    assert len(insts) == o.ndma, (o.idx, len(insts), o.ndma)
                    for i in insts:
                        i.then_inc(sems[o.semkey], 16)
                elif o.marked:
                    inst.then_inc(sems[o.semkey], 1)
        return run

    with nc.Block() as block:
        for eng in ("sync", "tensor", "scalar", "vector", "gpsimd"):
            if eng in per_eng:
                getattr(block, eng)(runner(per_eng[eng]))
    es.close()
    ctx.close()
    return nc


_CACHED = {}


def kernel(**inputs):
    if "prog" not in _CACHED:
        _CACHED["prog"] = build_program()
    nc = _CACHED["prog"]
    in_maps = [_host_core_inputs(c, inputs) for c in range(8)]
    res = run_bass_kernel_spmd(nc, in_maps, core_ids=list(range(8)))
    out = np.zeros((B, S, D), np.float32)
    for c in range(8):
        b, half = c // 2, c % 2
        y = res.results[c]["y"]
        for p, g in enumerate(_pos_glist(half)):
            out[b, g * 128:(g + 1) * 128] = y[p * 128:(p + 1) * 128]
    return out


# version 02 (final)
# speedup vs baseline: 1.1421x; 1.1421x over previous
import contextlib
import numpy as np
import ml_dtypes
import concourse.bass as bass
import concourse.mybir as mybir
from concourse.bass_utils import run_bass_kernel_spmd

F32 = mybir.dt.float32
BF16 = mybir.dt.bfloat16
I32 = mybir.dt.int32
I16 = mybir.dt.int16
AF = mybir.ActivationFunctionType
ALU = mybir.AluOpType
AXX = mybir.AxisListType.X

B, S, D = 4, 4096, 2048
Hi, Di, H, hd = 16, 128, 16, 128
TOPK = 256
INTER = 5504
NOC = INTER // 128
NPOS = 16
NKC = D // 128
NITER = 28
NEGC = -2e36
RT = float(hd) ** -0.5


def _own_tiles(half):
    return [g for g in range(32) if (g % 4) in ((0, 3) if half == 0 else (1, 2))]


def _pos_glist(half):
    gs = _own_tiles(half)
    cnt = lambda g: (g + 4) // 4
    bigs = sorted(gs, key=lambda g: (-cnt(g), g))[:8]
    smalls = sorted(gs, key=lambda g: (cnt(g), g))[:8]
    order = []
    for i in range(8):
        order.append(bigs[i]); order.append(smalls[i])
    return order


CNT = [(g + 4) // 4 for g in _pos_glist(0)]
assert CNT == [(g + 4) // 4 for g in _pos_glist(1)] and sum(CNT) == 72


class _Op:
    __slots__ = ("idx", "eng", "fn", "waits", "dma", "marked", "semkey", "count", "waitvals", "ndma")


class Res:
    __slots__ = ("w", "r")

    def __init__(self):
        self.w = None
        self.r = []


class Sched:
    def __init__(self):
        self.ops = []

    def add(self, eng, fn, reads=(), writes=(), waits=(), dma=False, ndma=1):
        o = _Op()
        o.idx = len(self.ops); o.eng = eng; o.fn = fn; o.dma = dma
        o.ndma = ndma
        o.marked = False
        o.semkey = ("q_" + eng) if dma else eng
        o.count = None; o.waitvals = None
        w = set()
        for r in reads:
            if r.w is not None:
                w.add(r.w)
        for res in writes:
            if res.w is not None:
                w.add(res.w)
            w.update(res.r)
        for x in waits:
            if x is not None:
                w.add(x.idx if isinstance(x, _Op) else int(x))
        o.waits = w
        self.ops.append(o)
        for r in reads:
            r.r.append(o.idx)
        for res in writes:
            res.w = o.idx
            res.r = []
        return o

    def finalize(self):
        ops = self.ops

        def _skip(p, o):
            return (p.eng == o.eng == "tensor") and not p.dma and not o.dma

        for o in ops:
            if o.dma:
                o.marked = True
            for wi in o.waits:
                p = ops[wi]
                if _skip(p, o):
                    continue
                p.marked = True
        counts = {}
        for o in ops:
            if o.marked:
                amt = (16 * o.ndma) if o.dma else 1
                counts[o.semkey] = counts.get(o.semkey, 0) + amt
                o.count = counts[o.semkey]
        for o in ops:
            wv = {}
            if o.dma:
                prev = o.count - (16 * o.ndma)
                if prev > 0:
                    wv[o.semkey] = prev
            for wi in o.waits:
                p = ops[wi]
                if not p.marked or _skip(p, o):
                    continue
                wv[p.semkey] = max(wv.get(p.semkey, 0), p.count)
            o.waitvals = wv
        seen = {}
        for o in ops:
            m = seen.setdefault(o.eng, {})
            wv = {}
            for k, v in o.waitvals.items():
                if m.get(k, -1) < v:
                    wv[k] = v
                    m[k] = v
            o.waitvals = wv
        return counts


def _hadamard(n):
    h = np.array([[1.0]])
    while h.shape[0] < n:
        h = np.block([[h, h], [h, -h]])
    return h * n ** (-0.5)


def _bf(a):
    return np.ascontiguousarray(np.asarray(a).astype(ml_dtypes.bfloat16))


def _tile_lhsT(w):
    rows, cols = w.shape
    kc, mc = rows // 128, cols // 128
    a = w.reshape(kc, 128, mc, 128)
    return np.ascontiguousarray(a.transpose(2, 1, 0, 3).reshape(mc * 128, kc * 128))


def _host_core_inputs(core, inputs):
    b, half = core // 2, core % 2
    glist = _pos_glist(half)
    x = inputs["L0_attn_x"][b]
    xq = np.concatenate([x[g * 128:(g + 1) * 128] for g in glist], 0)
    rowc = np.zeros((NPOS * 128, 8), np.float32)
    caps = np.zeros((NPOS * 128, 512), np.float32)
    c01 = np.zeros((NPOS * 128, 512), np.float32)
    for p, g in enumerate(glist):
        t = g * 128 + np.arange(128)
        n = (t + 1).astype(np.float64)
        short = n <= TOPK
        with np.errstate(divide="ignore", invalid="ignore"):
            k1 = np.sqrt(TOPK / np.maximum(n - TOPK, 1e-9))
            k2 = np.sqrt(np.maximum(n - TOPK, 0.0) / TOPK)
        sl = p * 128
        rowc[sl:sl + 128, 0] = 1.0 / (2.0 * n)
        rowc[sl:sl + 128, 6] = 1.0 / (4.0 * n)
        rowc[sl:sl + 128, 1] = np.where(short, 0.0, 1.0)
        rowc[sl:sl + 128, 2] = np.where(short, 0.0, -(1.05 * k1 + 1e-3))
        rowc[sl:sl + 128, 3] = np.where(short, NEGC, 0.0)
        rowc[sl:sl + 128, 4] = np.where(short, 0.0, (1.05 * k2 + 1e-3))
        rowc[sl:sl + 128, 5] = np.where(short, NEGC, 0.0)
        sg = (CNT[p] - 1) * 512 + np.arange(512)[None, :]
        causal = (t[:, None] >= sg)
        caps[sl:sl + 128] = np.where(causal, 3e38, -2e38)
        c01[sl:sl + 128] = np.where(causal, 1.0, 0.0)

    def tile_gu(idx):
        a = idx.reshape(NOC, 128, NKC, 128)
        return np.ascontiguousarray(a.transpose(0, 3, 2, 1).reshape(NOC * 128, NKC * 128))

    return {
        "xq": np.ascontiguousarray(xq),
        "xb": np.ascontiguousarray(x),
        "wqi_t": _tile_lhsT(inputs["L0_attn_wq_idx"]),
        "wki_t": _tile_lhsT(inputs["L0_attn_wk_idx"]),
        "wq_t": _tile_lhsT(inputs["L0_attn_wq"]),
        "w_iw": inputs["L0_attn_w_iw"],
        "wk": inputs["L0_attn_wk"],
        "wv": inputs["L0_attn_wv"],
        "wo": inputs["L0_attn_wo"],
        "lutv": np.ascontiguousarray(inputs["L0_ffn_lut"].reshape(1, 16)),
        "idxg": tile_gu(inputs["L0_ffn_gate_idx"]),
        "idxu": tile_gu(inputs["L0_ffn_up_idx"]),
        "idxd": np.ascontiguousarray(inputs["L0_ffn_down_idx"].T),
        "rowc": rowc,
        "caps": _bf(caps),
        "c01": _bf(c01),
        "hs": _bf(_hadamard(Di)),
        "ident": _bf(np.eye(128)),
        "ones": _bf(np.ones((128, 128))),
    }


def build_program(debug=()):
    debug = set(debug)
    nc = bass.Bass("TRN2", target_bir_lowering=False, debug=False)
    sch = Sched()
    A = sch.add

    din = {}
    for name, shape, dt in [
        ("xq", (NPOS * 128, D), F32), ("xb", (S, D), F32),
        ("wqi_t", (D, D), F32), ("wki_t", (Di, D), F32), ("wq_t", (D, D), F32),
        ("w_iw", (D, Hi), F32), ("wk", (D, hd), F32), ("wv", (D, hd), F32),
        ("wo", (D, D), F32), ("lutv", (1, 16), F32),
        ("idxg", (NOC * 128, D), I32), ("idxu", (NOC * 128, D), I32),
        ("idxd", (INTER, D), I32),
        ("rowc", (NPOS * 128, 8), F32),
        ("caps", (NPOS * 128, 512), BF16), ("c01", (NPOS * 128, 512), BF16),
        ("hs", (128, 128), BF16), ("ident", (128, 128), BF16), ("ones", (128, 128), BF16),
    ]:
        din[name] = nc.dram_tensor(name, list(shape), dt, kind="ExternalInput").ap()
    y_out = nc.dram_tensor("y", [NPOS * 128, D], F32, kind="ExternalOutput").ap()
    q_scr = nc.dram_tensor("q_scr", [NPOS * 128, D], BF16).ap()
    x2_scr = nc.dram_tensor("x2_scr", [NPOS * 128, D], F32).ap()
    h_scr = nc.dram_tensor("h_scr", [INTER, D], BF16).ap()
    wo_scr = nc.dram_tensor("wo_scr", [D, D], BF16).ap()
    wd_scr = nc.dram_tensor("wd_scr", [INTER, D], BF16).ap()
    devnull = nc.dram_tensor("devnull", [128, 4096], F32).ap()
    dbg = {}
    for name, shape, dt in [
        ("dbg_kT", (128, S), BF16), ("dbg_kiT", (128, S), BF16),
        ("dbg_v", (128, 32 * 128), BF16), ("dbg_qiT", (128, 16 * 2048), BF16),
        ("dbg_iw", (128, 256), F32), ("dbg_I", (128, 4096), F32),
        ("dbg_stat", (128, 64), F32), ("dbg_lo2", (128, NPOS), F32),
        ("dbg_mask", (128, 4096), BF16), ("dbg_oT", (128, 16 * 2048), BF16),
        ("dbg_x2", (NPOS * 128, D), F32), ("dbg_q", (NPOS * 128, D), BF16),
        ("dbg_maskT", (128, 4096), BF16), ("dbg_r", (128, 2048), F32),
        ("dbg_est", (128, 512), BF16), ("dbg_pst", (128, 512), BF16),
        ("dbg_rcp", (128, 512), F32), ("dbg_oacc", (128, 512), F32),
        ("dbg_pst4", (128, 4 * 512), BF16),
        ("dbg_sc", (128, 512), F32),
        ("dbg_wg", (128, 2048), BF16),
    ]:
        if name in debug:
            dbg[name] = nc.dram_tensor(name, list(shape), dt, kind="ExternalOutput").ap()

    ctx = contextlib.ExitStack()
    sb = lambda name, shape, dt: ctx.enter_context(nc.sbuf_tensor(name, list(shape), dt))

    BUF_A = sb("BUF_A", (128, 16 * 2048), BF16)
    IBUF = [sb("IBUF0", (128, 4096), F32), sb("IBUF1", (128, 2048), F32)]
    JNK = sb("JNK", (128, 4096), BF16)
    kT = sb("kT", (128, S), BF16)
    kiT = sb("kiT", (128, S), BF16)
    v_sb = sb("v_sb", (128, 32 * 128), BF16)
    mask_ts = sb("mask_ts", (128, 4096), BF16)
    maskTT = sb("maskTT", (128, 8192), BF16)
    maskT = [maskTT[:, :4096], maskTT[:, 4096:]]
    qstage = [sb(f"qstage{i}", (128, 2048), BF16) for i in range(2)]
    qT_sb = [sb(f"qT{i}", (128, 2048), BF16) for i in range(2)]
    diag_sb = sb("diag_sb", (128, 16 * 128), BF16)
    Rst = [sb(f"Rst{i}", (128, 512), BF16) for i in range(2)]
    est = [sb(f"est{i}", (128, 512), BF16) for i in range(2)]
    pst = [sb(f"pst{i}", (128, 512), BF16) for i in range(2)]
    recip_t = sb("recip_t", (128, 1024), F32)
    recip = [recip_t[:, :512], recip_t[:, 512:]]
    i0scr = sb("i0scr", (128, 512), F32)
    flsh = sb("flsh", (128, 8), BF16)
    sqscr = sb("sqscr", (128, 512), F32)
    iw_sb = sb("iw_sb", (128, 256), F32)
    stat = sb("stat", (128, 64), F32)
    smal = sb("smal", (128, 24), F32)
    gei = sb("gei", (128, 2), mybir.dt.uint8)
    rowc_sb = [sb(f"rowc{i}", (128, 8), F32) for i in range(2)]
    caps_sb = [sb(f"caps{i}", (128, 512), BF16) for i in range(2)]
    c01_sb = [sb(f"c01_{i}", (128, 512), BF16) for i in range(2)]
    hs_sb = sb("hs_sb", (128, 128), BF16)
    id_sb = sb("id_sb", (128, 128), BF16)
    on_sb = sb("on_sb", (128, 128), BF16)
    lut_sb = sb("lut_sb", (128, 16), F32)
    wiw_sb = sb("wiw_sb", (128, 256), BF16)
    wbig = [sb(f"wbig{i}", (128, 2048), BF16) for i in range(3)]
    st5 = [sb(f"st5_{i}", (128, 512), BF16) for i in range(2)]

    xTg = [IBUF[0][:].bitcast(BF16), maskTT[:]]
    xst = [qstage[0][:], qstage[1][:]]
    wsmk = [mask_ts[:, :2048], mask_ts[:, 2048:]]
    wkit = qT_sb[0][:]
    qtok = [mask_ts[:, :2048], mask_ts[:, 2048:],
            diag_sb[:], recip_t[:].bitcast(BF16)]
    xres = [IBUF[0][:, :2048], IBUF[0][:, 2048:]]
    rtile = kT[:].bitcast(F32)
    x2bst = kiT[:, :2048]
    ib16b = IBUF[1][:].bitcast(I16)
    idx_d = [ib16b[:, :2048], ib16b[:, 2048:4096]]
    idx_g = [JNK[:].bitcast(I16)[:, :2048], JNK[:].bitcast(I16)[:, 2048:4096]]
    idx_u = [IBUF[0][:].bitcast(I16)[:, :2048], IBUF[0][:].bitcast(I16)[:, 2048:4096]]
    Wgb = [mask_ts[:, :2048], mask_ts[:, 2048:]]
    Wub = [maskT[0][:, :2048], maskT[0][:, 2048:]]
    Wdb = [maskT[1][:, :2048], maskT[1][:, 2048:]]
    mtmp = [v_sb[:, :2048], v_sb[:, 2048:]]
    hTst = [kiT[:, :2048], kiT[:, 2048:]]
    hrd = [qstage[0][:], qstage[1][:]]
    wdrd = [qT_sb[0][:], qT_sb[1][:]]

    PSB = [ctx.enter_context(nc.psum_tensor(f"PS{i}", [128, 512], F32)) for i in range(8)]
    BANK = [t[:] for t in PSB]
    PD_bf = PSB[7][:].bitcast(BF16)
    RB = [Res() for _ in range(8)]

    r_A = [Res() for _ in range(16 * NPOS)]
    rA = lambda c, p: r_A[c * NPOS + p]
    r_I = [Res(), Res()]
    r_junk = r_I[0]
    r_kT = Res(); r_kiT = Res(); r_v = Res()
    r_mts = Res()
    r_qs = [Res(), Res()]; r_qT = [Res(), Res()]
    r_diag = Res(); r_R = [Res(), Res()]; r_e = [Res(), Res()]; r_p = [Res(), Res()]
    r_rc = [Res(), Res()]; r_i0 = Res(); r_sq = Res()
    r_iw = Res(); r_stat = Res(); r_smal = Res()
    r_rowc = [Res(), Res()]; r_caps = [Res(), Res()]; r_c01 = [Res(), Res()]
    r_const = Res(); r_wsm = Res()
    r_wbig = [Res() for _ in range(4)]
    r_st5 = [Res(), Res()]
    r_idxg = [Res(), Res()]; r_idxu = [Res(), Res()]; r_idxd = [Res(), Res()]
    r_hT = [Res(), Res()]; r_x2r = [Res(), Res()]
    r_xTg = [r_I[0], Res()]
    r_maskTT = r_xTg[1]
    r_mT = [r_maskTT, r_maskTT]
    r_xst = r_qs
    r_mtmp = [r_v, r_v]
    r_junk2 = Res()
    rd_q = Res(); rd_x2 = Res(); rd_h = Res(); rd_wo = Res(); rd_wd = Res()

    A("sync", lambda e: e.dma_start(out=hs_sb[:], in_=din["hs"]), writes=[r_const], dma=True)
    A("sync", lambda e: e.dma_start(out=id_sb[:], in_=din["ident"]), writes=[r_const], dma=True)
    A("sync", lambda e: e.dma_start(out=on_sb[:], in_=din["ones"]), writes=[r_const], dma=True)
    A("sync", lambda e: e.dma_start(out=lut_sb[:], in_=din["lutv"].partition_broadcast(128)),
      writes=[r_const], dma=True)

    r_wiwt = Res()
    def ld_small(e):
        out = []
        for c in range(NKC):
            out.append(e.dma_start(out=mask_ts[:, c * 128:(c + 1) * 128],
                                   in_=din["wk"][c * 128:(c + 1) * 128, :]))
        for c in range(NKC):
            out.append(e.dma_start(out=mask_ts[:, 2048 + c * 128:2048 + (c + 1) * 128],
                                   in_=din["wv"][c * 128:(c + 1) * 128, :]))
        return out
    A("gpsimd", ld_small, writes=[r_mts], dma=True, ndma=2 * NKC)
    A("gpsimd", (lambda e: [e.dma_start(out=wiw_sb[:, c * 16:(c + 1) * 16],
                                        in_=din["w_iw"][c * 128:(c + 1) * 128, :])
                            for c in range(NKC)]),
      writes=[r_wiwt], dma=True, ndma=NKC)
    wk_t = lambda c: mask_ts[:, c * 128:(c + 1) * 128]
    wv_t = lambda c: mask_ts[:, 2048 + c * 128:2048 + (c + 1) * 128]
    wiw_t = lambda c: wiw_sb[:, c * 16:(c + 1) * 16]
    A("gpsimd", lambda e: e.dma_start(out=wkit, in_=din["wki_t"][0:128, :]),
      writes=[r_qT[0]], dma=True)
    wki_t = lambda c: wkit[:, c * 128:(c + 1) * 128]

    def stage_tokens(src, row0, gslot, tig, xslot):
        ld = A("gpsimd", (lambda e, src=src, row0=row0, xslot=xslot:
                          e.dma_start(out=xst[xslot][:], in_=src[row0:row0 + 128, :])),
               writes=[r_xst[xslot]], dma=True)
        for c0 in (0, 8):
            tr = A("tensor", (lambda t, c0=c0, xslot=xslot:
                              [t.transpose(PD_bf[:, (c - c0) * 128:(c - c0 + 1) * 128],
                                           xst[xslot][:, c * 128:(c + 1) * 128], id_sb[:])
                               for c in range(c0, c0 + 8)][-1]),
                   reads=[r_xst[xslot], r_const], writes=[RB[7]], waits=[ld])
            A("scalar", (lambda a, c0=c0, gslot=gslot, tig=tig:
                         [a.copy(xTg[gslot][:, c * 512 + tig * 128: c * 512 + tig * 128 + 128],
                                 PD_bf[:, (c - c0) * 128:(c - c0 + 1) * 128])
                          for c in range(c0, c0 + 8)][-1]),
              reads=[RB[7]], writes=[r_xTg[gslot]], waits=[tr])

    for grp in range(8):
        gs = grp % 2
        for ti in range(4):
            stage_tokens(din["xb"], grp * 512 + ti * 128, gs, ti, ti % 2)
        mm = A("tensor", (lambda t, gs=gs:
                          [t.matmul(BANK[0], wk_t(c), xTg[gs][:, c * 512:(c + 1) * 512],
                                    start=(c == 0), stop=(c == NKC - 1), skip_group_check=True)
                           for c in range(NKC)][-1]),
               reads=[r_xTg[gs], r_mts], writes=[RB[0]])
        A("scalar", (lambda a, grp=grp: a.copy(kT[:, grp * 512:(grp + 1) * 512], BANK[0])),
          reads=[RB[0]], writes=[r_kT])
        mm = A("tensor", (lambda t, gs=gs:
                          [t.matmul(BANK[1], wki_t(c), xTg[gs][:, c * 512:(c + 1) * 512],
                                    start=(c == 0), stop=(c == NKC - 1), skip_group_check=True)
                           for c in range(NKC)][-1]),
               reads=[r_xTg[gs], r_qT[0]], writes=[RB[1]])
        A("scalar", (lambda a, gs=gs: a.copy(st5[gs][:], BANK[1])),
          reads=[RB[1]], writes=[r_st5[gs]])
        A("tensor", (lambda t, gs=gs: t.matmul(BANK[1], hs_sb[:], st5[gs][:],
                                               start=True, stop=True, skip_group_check=True)),
          reads=[r_st5[gs], r_const], writes=[RB[1]])
        A("scalar", (lambda a, grp=grp: a.copy(kiT[:, grp * 512:(grp + 1) * 512], BANK[1])),
          reads=[RB[1]], writes=[r_kiT])
        for ti in range(4):
            mm = A("tensor", (lambda t, gs=gs, ti=ti:
                              [t.matmul(BANK[4][:, :128],
                                        xTg[gs][:, c * 512 + ti * 128: c * 512 + ti * 128 + 128],
                                        wv_t(c), start=(c == 0), stop=(c == NKC - 1),
                                        skip_group_check=True)
                               for c in range(NKC)][-1]),
                   reads=[r_xTg[gs], r_mts], writes=[RB[4]])
            A("scalar", (lambda a, grp=grp, ti=ti:
                         a.copy(v_sb[:, (grp * 4 + ti) * 128:(grp * 4 + ti + 1) * 128],
                                BANK[4][:, :128])),
              reads=[RB[4]], writes=[r_v])

    for c in range(NKC):
        s = c % 2
        ld = A("gpsimd", (lambda e, c=c, s=s:
                          e.dma_start(out=wbig[s][:], in_=din["wo"][c * 128:(c + 1) * 128, :])),
               writes=[r_wbig[s]], dma=True)
        A("sync", (lambda e, c=c, s=s:
                   e.dma_start(out=wo_scr[c * 128:(c + 1) * 128, :], in_=wbig[s][:])),
          reads=[r_wbig[s]], writes=[rd_wo], waits=[ld], dma=True)

    for grp in range(4):
        gs = grp % 2
        for ti in range(4):
            stage_tokens(din["xq"], grp * 512 + ti * 128, gs, ti, ti % 2)
        for ti in range(4):
            p = grp * 4 + ti
            mm = A("tensor", (lambda t, gs=gs, ti=ti:
                              [t.matmul(BANK[5][:, :16],
                                        xTg[gs][:, c * 512 + ti * 128: c * 512 + ti * 128 + 128],
                                        wiw_t(c), start=(c == 0), stop=(c == NKC - 1),
                                        skip_group_check=True)
                               for c in range(NKC)][-1]),
                   reads=[r_xTg[gs], r_wiwt], writes=[RB[5]])
            A("vector", (lambda ve, p=p: ve.tensor_copy(iw_sb[:, p * 16:(p + 1) * 16],
                                                        BANK[5][:, :16])),
              reads=[RB[5]], writes=[r_iw])
        for hc in range(16):
            s = hc % 2
            ldi = A("gpsimd", (lambda e, hc=hc, s=s:
                               e.dma_start(out=wbig[s][:], in_=din["wqi_t"][hc * 128:(hc + 1) * 128, :])),
                    writes=[r_wbig[s]], dma=True)
            A("tensor", (lambda t, gs=gs, s=s:
                         [t.matmul(BANK[0], wbig[s][:, c * 128:(c + 1) * 128],
                                   xTg[gs][:, c * 512:(c + 1) * 512],
                                   start=(c == 0), stop=(c == NKC - 1), skip_group_check=True)
                          for c in range(NKC)][-1]),
              reads=[r_xTg[gs], r_wbig[s]], writes=[RB[0]], waits=[ldi])
            A("scalar", (lambda a, s=s: a.copy(st5[s][:], BANK[0])),
              reads=[RB[0]], writes=[r_st5[s]])
            A("tensor", (lambda t, s=s: t.matmul(BANK[1], hs_sb[:], st5[s][:],
                                                 start=True, stop=True, skip_group_check=True)),
              reads=[r_st5[s], r_const], writes=[RB[1]])
            A("scalar", (lambda a, hc=hc, grp=grp:
                         [a.copy(BUF_A[:, hc * 2048 + (grp * 4 + ti) * 128:
                                 hc * 2048 + (grp * 4 + ti) * 128 + 128],
                                 BANK[1][:, ti * 128:(ti + 1) * 128])
                          for ti in range(4)][-1]),
              reads=[RB[1]], writes=[rA(hc, grp * 4 + ti2) for ti2 in range(4)])
            ldq = A("gpsimd", (lambda e, hc=hc:
                               e.dma_start(out=wbig[2][:], in_=din["wq_t"][hc * 128:(hc + 1) * 128, :])),
                    writes=[r_wbig[2]], dma=True)
            A("tensor", (lambda t, gs=gs:
                         [t.matmul(BANK[2], wbig[2][:, c * 128:(c + 1) * 128],
                                   xTg[gs][:, c * 512:(c + 1) * 512],
                                   start=(c == 0), stop=(c == NKC - 1), skip_group_check=True)
                          for c in range(NKC)][-1]),
              reads=[r_xTg[gs], r_wbig[2]], writes=[RB[2]], waits=[ldq])
            A("scalar", (lambda a, s=s: a.copy(st5[s][:], BANK[2])),
              reads=[RB[2]], writes=[r_st5[s]])
            A("tensor", (lambda t, s=s:
                         [t.transpose(PD_bf[:, i * 128:(i + 1) * 128],
                                      st5[s][:, i * 128:(i + 1) * 128], id_sb[:])
                          for i in range(4)][-1]),
              reads=[r_st5[s], r_const], writes=[RB[7]])
            A("scalar", (lambda a, hc=hc:
                         [a.copy(qtok[ti][:, hc * 128:(hc + 1) * 128],
                                 PD_bf[:, ti * 128:(ti + 1) * 128])
                          for ti in range(4)][-1]),
              reads=[RB[7]], writes=[r_mts, r_diag, r_rc[0], r_rc[1]])
        for ti in range(4):
            p = grp * 4 + ti
            A("sync", (lambda e, p=p, ti=ti:
                       e.dma_start(out=q_scr[p * 128:(p + 1) * 128, :], in_=qtok[ti][:])),
              reads=[r_mts, r_diag, r_rc[0], r_rc[1]], writes=[rd_q], dma=True)

    if "dbg_kT" in dbg:
        A("sync", lambda e: e.dma_start(out=dbg["dbg_kT"], in_=kT[:]), reads=[r_kT], dma=True)
    if "dbg_kiT" in dbg:
        A("sync", lambda e: e.dma_start(out=dbg["dbg_kiT"], in_=kiT[:]), reads=[r_kiT], dma=True)
    if "dbg_v" in dbg:
        A("sync", lambda e: e.dma_start(out=dbg["dbg_v"], in_=v_sb[:]), reads=[r_v], dma=True)
    if "dbg_qiT" in dbg:
        A("sync", lambda e: e.dma_start(out=dbg["dbg_qiT"], in_=BUF_A[:]),
          reads=[rA(c, p) for c in range(16) for p in range(NPOS)], dma=True)
    if "dbg_iw" in dbg:
        A("sync", lambda e: e.dma_start(out=dbg["dbg_iw"], in_=iw_sb[:]), reads=[r_iw], dma=True)
    if "dbg_q" in dbg:
        A("sync", lambda e: e.dma_start(out=dbg["dbg_q"], in_=q_scr), reads=[rd_q], dma=True)

    if debug and not (debug - {"dbg_kT", "dbg_kiT", "dbg_v", "dbg_qiT", "dbg_iw", "dbg_q"}):
        return _finish(nc, sch, ctx)

    SL_SI, SL_SQ, SL_MU, SL_MS, SL_VAR, SL_T1, SL_T2 = 0, 1, 2, 3, 4, 5, 6
    SL_LO = [7, 8]; SL_HI = [9, 10]; SL_MS2, SL_CNT, SL_GE = 11, 12, 13
    sm = lambda c: smal[:, c:c + 1]

    def emit_importance(p):
        ib = p % 2
        E = 512 * CNT[p]
        ec = CNT[p] - 1
        ibuf = IBUF[ib]
        A("sync", (lambda e, p=p, ib=ib:
                   e.dma_start(out=rowc_sb[ib][:], in_=din["rowc"][p * 128:(p + 1) * 128, :])),
          writes=[r_rowc[ib]], dma=True)
        A("sync", (lambda e, p=p, ib=ib:
                   e.dma_start(out=caps_sb[ib][:], in_=din["caps"][p * 128:(p + 1) * 128, :])),
          writes=[r_caps[ib]], dma=True)
        A("sync", (lambda e, p=p, ib=ib:
                   e.dma_start(out=c01_sb[ib][:], in_=din["c01"][p * 128:(p + 1) * 128, :])),
          writes=[r_c01[ib]], dma=True)
        A("vector", (lambda ve, p=p:
                     [ve.tensor_scalar(diag_sb[:, h * 128:(h + 1) * 128], id_sb[:],
                                       iw_sb[:, p * 16 + h:p * 16 + h + 1], None, ALU.mult)
                      for h in range(16)][-1]),
          reads=[r_iw, r_const], writes=[r_diag])
        for c in range(CNT[p]):
            for h in range(16):
                pg = h % 2
                A("tensor", (lambda t, h=h, p=p, c=c, pg=pg:
                             t.matmul(BANK[pg], BUF_A[:, h * 2048 + p * 128: h * 2048 + p * 128 + 128],
                                      kiT[:, c * 512:(c + 1) * 512],
                                      start=True, stop=True, skip_group_check=True)),
                  reads=[rA(h, p), r_kiT], writes=[RB[pg]])
                A("scalar", (lambda a, pg=pg: a.activation(Rst[pg][:], BANK[pg], AF.Relu)),
                  reads=[RB[pg]], writes=[r_R[pg]])
                A("tensor", (lambda t, h=h, pg=pg:
                             t.matmul(BANK[2], diag_sb[:, h * 128:(h + 1) * 128], Rst[pg][:],
                                      start=(h == 0), stop=(h == 15), skip_group_check=True)),
                  reads=[r_R[pg], r_diag],
                  writes=[RB[2]] if h in (0, 15) else [])
            if c == ec:
                A("vector", (lambda ve, c=c, ib=ib:
                             ve.scalar_tensor_tensor(i0scr[:], BANK[2], 2.0, c01_sb[ib][:],
                                                     ALU.mult, ALU.mult,
                                                     accum_out=stat[:, c:c + 1])),
                  reads=[RB[2], r_c01[ib]], writes=[r_i0, r_stat])
                A("scalar", (lambda a, c=c: a.activation(sqscr[:], i0scr[:], AF.Square,
                                                         accum_out=stat[:, 8 + c:9 + c])),
                  reads=[r_i0], writes=[r_sq, r_stat])
                A("vector", (lambda ve, c=c, ib=ib, ibuf=ibuf:
                             ve.scalar_tensor_tensor(ibuf[:, c * 512:(c + 1) * 512], BANK[2],
                                                     2.0, caps_sb[ib][:], ALU.mult, ALU.min)),
                  reads=[RB[2], r_caps[ib]], writes=[r_I[ib]])
            else:
                A("vector", (lambda ve, c=c, ibuf=ibuf:
                             ve.tensor_scalar(ibuf[:, c * 512:(c + 1) * 512], BANK[2],
                                              2.0, None, ALU.mult, ALU.add,
                                              accum_out=stat[:, c:c + 1])),
                  reads=[RB[2]], writes=[r_I[ib], r_stat])
                A("scalar", (lambda a, c=c, ibuf=ibuf:
                             a.activation(sqscr[:], ibuf[:, c * 512:(c + 1) * 512], AF.Square,
                                          accum_out=stat[:, 8 + c:9 + c])),
                  reads=[r_I[ib]], writes=[r_sq, r_stat])
        n = CNT[p]
        A("vector", (lambda ve, n=n: ve.reduce_sum(sm(SL_SI), stat[:, 0:n], axis=AXX)),
          reads=[r_stat], writes=[r_smal])
        A("vector", (lambda ve, n=n: ve.reduce_sum(sm(SL_SQ), stat[:, 8:8 + n], axis=AXX)),
          reads=[r_stat], writes=[r_smal])
        A("vector", (lambda ve, ib=ib:
                     ve.tensor_scalar(sm(SL_MU), sm(SL_SI), rowc_sb[ib][:, 0:1], None, ALU.mult)),
          reads=[r_rowc[ib]], writes=[r_smal])
        A("vector", (lambda ve, ib=ib:
                     ve.tensor_scalar(sm(SL_MS), sm(SL_SQ), rowc_sb[ib][:, 6:7], None, ALU.mult)),
          reads=[r_rowc[ib]], writes=[r_smal])
        A("vector", lambda ve: ve.tensor_tensor(sm(SL_T1), sm(SL_MU), sm(SL_MU), ALU.mult),
          writes=[r_smal])
        A("vector", lambda ve: ve.tensor_tensor(sm(SL_VAR), sm(SL_MS), sm(SL_T1), ALU.subtract),
          writes=[r_smal])
        A("vector", lambda ve: ve.tensor_scalar(sm(SL_VAR), sm(SL_VAR), 1e-20, None, ALU.max),
          writes=[r_smal])
        A("scalar", lambda a: a.activation(sm(SL_T1), sm(SL_VAR), AF.Ln),
          reads=[r_smal], writes=[r_smal])
        A("scalar", lambda a: a.activation(sm(SL_T2), sm(SL_T1), AF.Exp, scale=0.5),
          writes=[r_smal])
        A("vector", (lambda ve, ib=ib:
                     ve.tensor_scalar(sm(SL_T1), sm(SL_MU), rowc_sb[ib][:, 1:2], None, ALU.mult)),
          writes=[r_smal])
        A("vector", (lambda ve, ib=ib:
                     ve.scalar_tensor_tensor(sm(SL_LO[0]), sm(SL_T2), rowc_sb[ib][:, 2:3],
                                             sm(SL_T1), ALU.mult, ALU.add)),
          writes=[r_smal])
        A("vector", (lambda ve, ib=ib:
                     ve.tensor_scalar(sm(SL_LO[0]), sm(SL_LO[0]), rowc_sb[ib][:, 3:4], None, ALU.add)),
          writes=[r_smal])
        A("vector", (lambda ve, ib=ib:
                     ve.scalar_tensor_tensor(sm(SL_HI[0]), sm(SL_T2), rowc_sb[ib][:, 4:5],
                                             sm(SL_T1), ALU.mult, ALU.add)),
          writes=[r_smal])
        A("vector", (lambda ve, ib=ib:
                     ve.tensor_scalar(sm(SL_HI[0]), sm(SL_HI[0]), rowc_sb[ib][:, 5:6], None, ALU.add)),
          writes=[r_smal])
        for it in range(NITER):
            pa, pb = it % 2, 1 - it % 2
            A("vector", (lambda ve, pa=pa: ve.tensor_tensor(sm(SL_MS2), sm(SL_LO[pa]),
                                                            sm(SL_HI[pa]), ALU.add)),
              writes=[r_smal])
            A("vector", (lambda ve, E=E, ibuf=ibuf:
                         ve.tensor_scalar(JNK[:, :E], ibuf[:, :E], sm(SL_MS2), None,
                                          ALU.is_ge, ALU.add, accum_out=sm(SL_CNT))),
              reads=[r_I[ib]], writes=[r_junk2, r_smal])
            A("vector", lambda ve: ve.tensor_scalar(gei[:, 0:1], sm(SL_CNT), float(TOPK), None,
                                                    ALU.is_ge),
              writes=[r_smal])
            A("vector", lambda ve: ve.tensor_scalar(sm(SL_MS2), sm(SL_MS2), 0.5, None, ALU.mult),
              writes=[r_smal])
            A("vector", (lambda ve, pa=pa, pb=pb:
                         ve.select(sm(SL_LO[pb]), gei[:, 0:1], sm(SL_MS2), sm(SL_LO[pa]))),
              writes=[r_smal])
            A("vector", (lambda ve, pa=pa, pb=pb:
                         ve.select(sm(SL_HI[pb]), gei[:, 0:1], sm(SL_HI[pa]), sm(SL_MS2))),
              writes=[r_smal])
        fin = NITER % 2
        A("vector", (lambda ve, fin=fin:
                     ve.tensor_tensor(sm(SL_MS2), sm(SL_LO[fin]), sm(SL_LO[fin]), ALU.add)),
          writes=[r_smal])
        if "dbg_lo2" in dbg:
            A("vector", (lambda ve, p=p, fin=fin:
                         ve.tensor_copy(stat[:, 32 + p:33 + p], sm(SL_LO[fin]))),
              writes=[r_stat])
        mko = A("vector", (lambda ve, E=E, ibuf=ibuf:
                           ve.tensor_scalar(mask_ts[:, :E], ibuf[:, :E], sm(SL_MS2), None,
                                            ALU.is_ge)),
                reads=[r_I[ib], r_smal], writes=[r_mts])
        if "dbg_I" in dbg and p == 1:
            A("sync", (lambda e, E=E, ibuf=ibuf: e.dma_start(out=dbg["dbg_I"][:, :E],
                                                             in_=ibuf[:, :E])),
              reads=[r_I[ib]], dma=True)
        if "dbg_mask" in dbg and p == 1:
            A("sync", (lambda e, E=E: e.dma_start(out=dbg["dbg_mask"][:, :E], in_=mask_ts[:, :E])),
              reads=[r_mts], waits=[mko], dma=True)
        if "dbg_stat" in dbg and p == 1:
            A("sync", lambda e: e.dma_start(out=dbg["dbg_stat"], in_=stat[:]),
              reads=[r_stat], dma=True)
        A("sync", (lambda e, E=E, ibuf=ibuf: e.dma_start(out=devnull[:, :E], in_=ibuf[:, :E])),
          reads=[r_I[ib]], dma=True)
        A("sync", (lambda e, E=E: e.dma_start(out=devnull[:].bitcast(BF16)[:, :E],
                                              in_=mask_ts[:, :E])),
          reads=[r_mts], waits=[mko], dma=True)
        mslot = p % 2
        for s0 in range(0, E // 128, 8):
            n8 = min(8, E // 128 - s0)
            A("tensor", (lambda t, s0=s0, n8=n8:
                         [t.transpose(PD_bf[:, i * 128:(i + 1) * 128],
                                      mask_ts[:, (s0 + i) * 128:(s0 + i + 1) * 128], id_sb[:])
                          for i in range(n8)][-1]),
              reads=[r_mts, r_const], writes=[RB[7]])
            A("scalar", (lambda a, s0=s0, n8=n8, mslot=mslot:
                         [a.copy(maskT[mslot][:, (s0 + i) * 128:(s0 + i + 1) * 128],
                                 PD_bf[:, i * 128:(i + 1) * 128])
                          for i in range(n8)][-1]),
              reads=[RB[7]], writes=[r_mT[mslot]])
        if "dbg_maskT" in dbg and p == 1:
            A("sync", (lambda e: e.dma_start(out=dbg["dbg_maskT"], in_=maskT[1][:, :4096])),
              reads=[r_mT[1]], dma=True)
        lq = A("sync", (lambda e, p=p, mslot=mslot:
                        e.dma_start(out=qstage[mslot][:], in_=q_scr[p * 128:(p + 1) * 128, :])),
               reads=[rd_q], writes=[r_qs[mslot]], dma=True)
        for c0 in (0, 8):
            A("tensor", (lambda t, c0=c0, mslot=mslot:
                         [t.transpose(PD_bf[:, (c - c0) * 128:(c - c0 + 1) * 128],
                                      qstage[mslot][:, c * 128:(c + 1) * 128], id_sb[:])
                          for c in range(c0, c0 + 8)][-1]),
              reads=[r_qs[mslot], r_const], writes=[RB[7]], waits=[lq])
            A("scalar", (lambda a, c0=c0, mslot=mslot:
                         [a.copy(qT_sb[mslot][:, c * 128:(c + 1) * 128],
                                 PD_bf[:, (c - c0) * 128:(c - c0 + 1) * 128])
                          for c in range(c0, c0 + 8)][-1]),
              reads=[RB[7]], writes=[r_qT[mslot]])

    def emit_attention(p):
        mslot = p % 2
        NS = 4 * CNT[p]
        for qtr in range(4):
            for sbk in range(NS):
                pg = sbk % 2
                A("tensor", (lambda t, sbk=sbk, qtr=qtr, pg=pg, mslot=mslot:
                             t.matmul(BANK[3 + pg], kT[:, sbk * 128:(sbk + 1) * 128],
                                      qT_sb[mslot][:, qtr * 512:(qtr + 1) * 512],
                                      start=True, stop=True, skip_group_check=True)),
                  reads=[r_kT, r_qT[mslot]], writes=[RB[3 + pg]])
                A("scalar", (lambda a, pg=pg: a.activation(est[pg][:], BANK[3 + pg],
                                                           AF.Exp, scale=RT)),
                  reads=[RB[3 + pg]], writes=[r_e[pg]])
                A("gpsimd", (lambda g, pg=pg, sbk=sbk, mslot=mslot:
                             g.tensor_tensor(
                                 pst[pg][:].rearrange("p (h t) -> p h t", h=4),
                                 est[pg][:].rearrange("p (h t) -> p h t", h=4),
                                 maskT[mslot][:, sbk * 128:(sbk + 1) * 128]
                                 .rearrange("p (o t) -> p o t", o=1)
                                 .to_broadcast([128, 4, 128]),
                                 ALU.mult)),
                  reads=[r_e[pg], r_mT[mslot]], writes=[r_p[pg]])
                if "dbg_est" in dbg and p == 1 and qtr == 2 and sbk == 0:
                    A("sync", (lambda e, pg=pg: e.dma_start(out=dbg["dbg_est"], in_=est[pg][:])),
                      reads=[r_e[pg]], dma=True)
                    A("sync", (lambda e, pg=pg: e.dma_start(out=dbg["dbg_pst"], in_=pst[pg][:])),
                      reads=[r_p[pg]], dma=True)
                if "dbg_pst4" in dbg and p == 1 and qtr == 2:
                    A("sync", (lambda e, pg=pg, sbk=sbk:
                               e.dma_start(out=dbg["dbg_pst4"][:, sbk * 512:(sbk + 1) * 512],
                                           in_=pst[pg][:])),
                      reads=[r_p[pg]], dma=True)
                A("tensor", (lambda t, sbk=sbk, pg=pg, first=(sbk == 0), lastc=(sbk == NS - 1):
                             t.matmul(BANK[5], v_sb[:, sbk * 128:(sbk + 1) * 128], pst[pg][:],
                                      start=first, stop=lastc, skip_group_check=True)),
                  reads=[r_p[pg], r_v],
                  writes=[RB[5]] if sbk in (0, NS - 1) else [])
                A("tensor", (lambda t, pg=pg, first=(sbk == 0), lastc=(sbk == NS - 1):
                             t.matmul(BANK[6], on_sb[:], pst[pg][:],
                                      start=first, stop=lastc, skip_group_check=True)),
                  reads=[r_p[pg], r_const],
                  writes=[RB[6]] if sbk in (0, NS - 1) else [])
            if "dbg_oacc" in dbg and p == 1 and qtr == 2:
                oev = A("scalar", (lambda a: a.copy(sqscr[:], BANK[5])), reads=[RB[5]],
                        writes=[r_sq])
                A("sync", (lambda e: e.dma_start(out=dbg["dbg_oacc"], in_=sqscr[:])),
                  reads=[r_sq], waits=[oev], dma=True)
            A("vector", (lambda ve, qtr=qtr: ve.reciprocal(recip[qtr % 2][:], BANK[6])),
              reads=[RB[6]], writes=[r_rc[qtr % 2]])
            if "dbg_rcp" in dbg and p == 1 and qtr == 2:
                A("sync", (lambda e, qtr=qtr: e.dma_start(out=dbg["dbg_rcp"], in_=recip[qtr % 2][:])),
                  reads=[r_rc[qtr % 2]], dma=True)
            A("vector", (lambda ve, p=p, qtr=qtr:
                         ve.tensor_tensor(
                             BUF_A[:].rearrange("p (h t) -> p h t", h=16)
                             [:, qtr * 4:(qtr + 1) * 4, p * 128:(p + 1) * 128],
                             BANK[5].rearrange("p (h t) -> p h t", h=4),
                             recip[qtr % 2][:].rearrange("p (h t) -> p h t", h=4),
                             ALU.mult)),
              reads=[RB[5], r_rc[qtr % 2]],
              writes=[rA(qtr * 4 + i, p) for i in range(4)])

    for p in range(NPOS):
        emit_importance(p)
        if p >= 1:
            emit_attention(p - 1)
    emit_attention(NPOS - 1)

    if "dbg_oT" in dbg:
        A("sync", lambda e: e.dma_start(out=dbg["dbg_oT"], in_=BUF_A[:]),
          reads=[rA(c, p) for c in range(16) for p in range(NPOS)], dma=True)
    if "dbg_lo2" in dbg:
        A("sync", lambda e: e.dma_start(out=dbg["dbg_lo2"], in_=stat[:, 32:48]),
          reads=[r_stat], dma=True)
    if debug and not (debug - {"dbg_I", "dbg_mask", "dbg_lo2", "dbg_stat", "dbg_oT", "dbg_maskT"}):
        return _finish(nc, sch, ctx)

    for p in range(NPOS):
        s = p % 2
        lx = A("sync", (lambda e, p=p, s=s:
                        e.dma_start(out=xres[s], in_=din["xq"][p * 128:(p + 1) * 128, :])),
               writes=[r_junk], dma=True)
        mmlast = None
        for c in range(NKC):
            ws = c % 2
            ld = A("gpsimd", (lambda e, c=c, ws=ws:
                              e.dma_start(out=wbig[ws][:], in_=wo_scr[c * 128:(c + 1) * 128, :])),
                   reads=[rd_wo], writes=[r_wbig[ws]], dma=True)
            mmlast = A("tensor", (lambda t, c=c, p=p, ws=ws, first=(c == 0), lastc=(c == NKC - 1):
                                  [t.matmul(BANK[d], BUF_A[:, c * 2048 + p * 128: c * 2048 + p * 128 + 128],
                                            wbig[ws][:, d * 512:(d + 1) * 512],
                                            start=first, stop=lastc, skip_group_check=True)
                                   for d in range(4)][-1]),
                       reads=[rA(c, p), r_wbig[ws]],
                       writes=[RB[0], RB[1], RB[2], RB[3]] if c in (0, NKC - 1) else [],
                       waits=[ld])
        for d in range(4):
            A("vector", (lambda ve, d=d, s=s:
                         ve.scalar_tensor_tensor(rtile[:, d * 512:(d + 1) * 512], BANK[d], 0.0,
                                                 xres[s][:, d * 512:(d + 1) * 512],
                                                 ALU.bypass, ALU.add,
                                                 accum_out=stat[:, 16 + d:17 + d])),
              reads=[RB[d], r_junk], writes=[r_kT, r_stat], waits=[lx, mmlast])
            A("scalar", (lambda a, d=d:
                         a.activation(sqscr[:], rtile[:, d * 512:(d + 1) * 512], AF.Square,
                                      accum_out=stat[:, 20 + d:21 + d])),
              reads=[r_kT], writes=[r_sq, r_stat])
        A("vector", lambda ve: ve.reduce_sum(sm(SL_SI), stat[:, 16:20], axis=AXX),
          reads=[r_stat], writes=[r_smal])
        A("vector", lambda ve: ve.reduce_sum(sm(SL_SQ), stat[:, 20:24], axis=AXX),
          reads=[r_stat], writes=[r_smal])
        A("vector", lambda ve: ve.tensor_scalar(sm(SL_MU), sm(SL_SI), 1.0 / D, None, ALU.mult),
          writes=[r_smal])
        A("vector", lambda ve: ve.tensor_scalar(sm(SL_MS), sm(SL_SQ), 1.0 / D, None, ALU.mult),
          writes=[r_smal])
        A("vector", lambda ve: ve.tensor_tensor(sm(SL_T1), sm(SL_MU), sm(SL_MU), ALU.mult),
          writes=[r_smal])
        A("vector", lambda ve: ve.tensor_tensor(sm(SL_VAR), sm(SL_MS), sm(SL_T1), ALU.subtract),
          writes=[r_smal])
        A("vector", lambda ve: ve.tensor_scalar(sm(SL_VAR), sm(SL_VAR), 1e-5, None, ALU.add),
          writes=[r_smal])
        A("scalar", lambda a: a.activation(sm(SL_T1), sm(SL_VAR), AF.Ln),
          reads=[r_smal], writes=[r_smal])
        A("scalar", lambda a: a.activation(sm(SL_T2), sm(SL_T1), AF.Exp, scale=-0.5),
          writes=[r_smal])
        if "dbg_r" in dbg and p == 1:
            A("sync", (lambda e: e.dma_start(out=dbg["dbg_r"], in_=rtile[:])),
              reads=[r_kT], dma=True)
        x2o = A("vector", (lambda ve, s=s:
                           ve.tensor_scalar(xres[s], rtile[:], sm(SL_MU), sm(SL_T2),
                                            ALU.subtract, ALU.mult)),
                reads=[r_kT, r_smal], writes=[r_junk])
        A("sync", (lambda e, p=p, s=s:
                   e.dma_start(out=x2_scr[p * 128:(p + 1) * 128, :], in_=xres[s])),
          reads=[r_junk], writes=[rd_x2], waits=[x2o], dma=True)
        cb = A("scalar", (lambda a, s=s: a.copy(x2bst[:], xres[s])),
               reads=[r_junk], writes=[r_kiT])
        for c0 in (0, 8):
            A("tensor", (lambda t, c0=c0:
                         [t.transpose(PD_bf[:, (c - c0) * 128:(c - c0 + 1) * 128],
                                      x2bst[:, c * 128:(c + 1) * 128], id_sb[:])
                          for c in range(c0, c0 + 8)][-1]),
              reads=[r_kiT, r_const], writes=[RB[7]], waits=[cb])
            A("scalar", (lambda a, c0=c0, p=p:
                         [a.copy(BUF_A[:, (c0 + i) * 2048 + p * 128: (c0 + i) * 2048 + p * 128 + 128],
                                 PD_bf[:, i * 128:(i + 1) * 128])
                          for i in range(8)][-1]),
              reads=[RB[7]], writes=[rA(c0 + i, p) for i in range(8)])

    if "dbg_x2" in dbg:
        A("sync", lambda e: e.dma_start(out=dbg["dbg_x2"], in_=x2_scr), reads=[rd_x2], dma=True)
    if debug and "dbg_x2" in dbg and "dbg_wg" not in dbg:
        return _finish(nc, sch, ctx)

    lut_s = lambda j: lut_sb[:, j:j + 1]

    def materialize(acc_ap, idx_ap, tmp_ap, r_acc, r_idx, r_tmp):
        A("vector", (lambda ve: ve.tensor_scalar(acc_ap, idx_ap, 0.0, lut_s(0),
                                                 ALU.is_equal, ALU.mult)),
          reads=[r_idx, r_const], writes=[r_acc])
        for j in range(1, 16):
            A("vector", (lambda ve, j=j: ve.tensor_scalar(tmp_ap, idx_ap, float(j), lut_s(j),
                                                          ALU.is_equal, ALU.mult)),
              reads=[r_idx, r_const], writes=[r_tmp])
            A("vector", (lambda ve: ve.tensor_tensor(acc_ap, acc_ap, tmp_ap, ALU.add)),
              reads=[r_tmp], writes=[r_acc])

    x2T_res = [rA(kc, p) for kc in range(16) for p in range(NPOS)]
    for oc in range(NOC):
        s = oc % 2
        A("gpsimd", (lambda e, oc=oc, s=s:
                     e.dma_start(out=idx_g[s], in_=din["idxg"][oc * 128:(oc + 1) * 128, :])),
          writes=[r_idxg[s], r_junk2], dma=True)
        A("gpsimd", (lambda e, oc=oc, s=s:
                     e.dma_start(out=idx_u[s], in_=din["idxu"][oc * 128:(oc + 1) * 128, :])),
          writes=[r_idxu[s], r_I[0]], dma=True)
        materialize(Wgb[s], idx_g[s], mtmp[0][:], r_mts, r_idxg[s], r_mtmp[0])
        materialize(Wub[s], idx_u[s], mtmp[1][:], r_mT[0], r_idxu[s], r_mtmp[1])
        for th in range(2):
            pp = (oc * 2 + th) % 2
            gpsu, upsu = (0, 1) if pp == 0 else (2, 3)
            mmg = A("tensor", (lambda t, s=s, th=th, gpsu=gpsu:
                               [t.matmul(BANK[2 * gpsu + half],
                                         Wgb[s][:, kc * 128:(kc + 1) * 128],
                                         BUF_A[:, kc * 2048 + th * 1024 + half * 512:
                                               kc * 2048 + th * 1024 + half * 512 + 512],
                                         start=(kc == 0), stop=(kc == NKC - 1),
                                         skip_group_check=True)
                                for half in range(2) for kc in range(NKC)][-1]),
                    reads=[r_mts] + x2T_res,
                    writes=[RB[2 * gpsu], RB[2 * gpsu + 1]])
            mmu = A("tensor", (lambda t, s=s, th=th, upsu=upsu:
                               [t.matmul(BANK[2 * upsu + half],
                                         Wub[s][:, kc * 128:(kc + 1) * 128],
                                         BUF_A[:, kc * 2048 + th * 1024 + half * 512:
                                               kc * 2048 + th * 1024 + half * 512 + 512],
                                         start=(kc == 0), stop=(kc == NKC - 1),
                                         skip_group_check=True)
                                for half in range(2) for kc in range(NKC)][-1]),
                    reads=[r_mT[0]] + x2T_res,
                    writes=[RB[2 * upsu], RB[2 * upsu + 1]])
            sl = A("scalar", (lambda a, gpsu=gpsu:
                              [a.activation(sqscr[:].bitcast(BF16)[:, half * 512:(half + 1) * 512],
                                            BANK[2 * gpsu + half], AF.Silu)
                               for half in range(2)][-1]),
                   reads=[RB[2 * gpsu], RB[2 * gpsu + 1]], writes=[r_sq], waits=[mmg])
            A("vector", (lambda ve, upsu=upsu, s=s, th=th:
                         [ve.tensor_tensor(hTst[s][:, th * 1024 + half * 512:th * 1024 + half * 512 + 512],
                                           sqscr[:].bitcast(BF16)[:, half * 512:(half + 1) * 512],
                                           BANK[2 * upsu + half], ALU.mult)
                          for half in range(2)][-1]),
              reads=[r_sq, RB[2 * upsu], RB[2 * upsu + 1]], writes=[r_hT[s]],
              waits=[mmu, sl])
        A("sync", (lambda e, oc=oc, s=s:
                   e.dma_start(out=h_scr[oc * 128:(oc + 1) * 128, :], in_=hTst[s][:])),
          reads=[r_hT[s]], writes=[rd_h], dma=True)
        if "dbg_wg" in dbg and oc == 0:
            A("sync", (lambda e, s=s: e.dma_start(out=dbg["dbg_wg"], in_=Wgb[s])),
              reads=[r_mts], dma=True)
    for ic in range(NOC):
        s = ic % 2
        A("gpsimd", (lambda e, ic=ic, s=s:
                     e.dma_start(out=idx_d[s], in_=din["idxd"][ic * 128:(ic + 1) * 128, :])),
          writes=[r_idxd[s], r_I[1]], dma=True)
        materialize(Wdb[s], idx_d[s], mtmp[s][:], r_mT[1], r_idxd[s], r_mtmp[s])
        A("sync", (lambda e, ic=ic, s=s:
                   e.dma_start(out=wd_scr[ic * 128:(ic + 1) * 128, :], in_=Wdb[s][:])),
          reads=[r_mT[1]], writes=[rd_wd], dma=True)

    if debug:
        return _finish(nc, sch, ctx)

    y_ops = []
    for pp in range(0, NPOS, 2):
        lx = []
        for j in range(2):
            lx.append(A("sync", (lambda e, pp=pp, j=j:
                                 e.dma_start(out=xres[j],
                                             in_=x2_scr[(pp + j) * 128:(pp + j + 1) * 128, :])),
                        reads=[rd_x2], writes=[r_x2r[j]], dma=True))
        mmlast = None
        for ic in range(NOC):
            s = ic % 2
            lh = A("sync", (lambda e, ic=ic, s=s:
                            e.dma_start(out=hrd[s], in_=h_scr[ic * 128:(ic + 1) * 128, :])),
                   reads=[rd_h], writes=[r_qs[s]], dma=True)
            lw = A("gpsimd", (lambda e, ic=ic, s=s:
                              e.dma_start(out=wdrd[s], in_=wd_scr[ic * 128:(ic + 1) * 128, :])),
                   reads=[rd_wd], writes=[r_qT[s]], dma=True)
            mmlast = A("tensor", (lambda t, ic=ic, s=s, pp=pp,
                                  first=(ic == 0), lastc=(ic == NOC - 1):
                                  [t.matmul(BANK[j * 4 + d],
                                            hrd[s][:, (pp + j) * 128:(pp + j + 1) * 128],
                                            wdrd[s][:, d * 512:(d + 1) * 512],
                                            start=first, stop=lastc, skip_group_check=True)
                                   for j in range(2) for d in range(4)][-1]),
                       reads=[r_qs[s], r_qT[s]],
                       writes=[RB[i] for i in range(8)] if ic in (0, NOC - 1) else [],
                       waits=[lh, lw])
        for j in range(2):
            p = pp + j
            for d in range(4):
                A("vector", (lambda ve, j=j, d=d:
                             ve.scalar_tensor_tensor(rtile[:, d * 512:(d + 1) * 512],
                                                     BANK[j * 4 + d], 0.0,
                                                     xres[j][:, d * 512:(d + 1) * 512],
                                                     ALU.bypass, ALU.add,
                                                     accum_out=stat[:, 16 + d:17 + d])),
                  reads=[RB[j * 4 + d], r_x2r[j]], writes=[r_kT, r_stat],
                  waits=[mmlast, lx[j]])
                A("scalar", (lambda a, d=d:
                             a.activation(sqscr[:], rtile[:, d * 512:(d + 1) * 512], AF.Square,
                                          accum_out=stat[:, 20 + d:21 + d])),
                  reads=[r_kT], writes=[r_sq, r_stat])
            A("vector", lambda ve: ve.reduce_sum(sm(SL_SI), stat[:, 16:20], axis=AXX),
              reads=[r_stat], writes=[r_smal])
            A("vector", lambda ve: ve.reduce_sum(sm(SL_SQ), stat[:, 20:24], axis=AXX),
              reads=[r_stat], writes=[r_smal])
            A("vector", lambda ve: ve.tensor_scalar(sm(SL_MU), sm(SL_SI), 1.0 / D, None, ALU.mult),
              writes=[r_smal])
            A("vector", lambda ve: ve.tensor_scalar(sm(SL_MS), sm(SL_SQ), 1.0 / D, None, ALU.mult),
              writes=[r_smal])
            A("vector", lambda ve: ve.tensor_tensor(sm(SL_T1), sm(SL_MU), sm(SL_MU), ALU.mult),
              writes=[r_smal])
            A("vector", lambda ve: ve.tensor_tensor(sm(SL_VAR), sm(SL_MS), sm(SL_T1), ALU.subtract),
              writes=[r_smal])
            A("vector", lambda ve: ve.tensor_scalar(sm(SL_VAR), sm(SL_VAR), 1e-5, None, ALU.add),
              writes=[r_smal])
            A("scalar", lambda a: a.activation(sm(SL_T1), sm(SL_VAR), AF.Ln),
              reads=[r_smal], writes=[r_smal])
            A("scalar", lambda a: a.activation(sm(SL_T2), sm(SL_T1), AF.Exp, scale=-0.5),
              writes=[r_smal])
            yo = A("vector", (lambda ve, j=j:
                              ve.tensor_scalar(xres[j], rtile[:], sm(SL_MU), sm(SL_T2),
                                               ALU.subtract, ALU.mult)),
                   reads=[r_kT, r_smal], writes=[r_x2r[j]])
            y_ops.append(A("sync", (lambda e, p=p, j=j:
                                    e.dma_start(out=y_out[p * 128:(p + 1) * 128, :], in_=xres[j])),
                           reads=[r_x2r[j]], waits=[yo], dma=True))
    A("sync", lambda e: e.nop(), waits=y_ops)
    return _finish(nc, sch, ctx)


def _finish(nc, sch, ctx):
    counts = sch.finalize()
    es = contextlib.ExitStack()
    sems = {k: es.enter_context(nc.semaphore(k)) for k in counts}
    per_eng = {}
    for o in sch.ops:
        per_eng.setdefault(o.eng, []).append(o)

    def runner(ops):
        def run(eng_obj):
            for o in ops:
                for k in sorted(o.waitvals):
                    eng_obj.wait_ge(sems[k], o.waitvals[k])
                inst = o.fn(eng_obj)
                if o.dma:
                    insts = inst if isinstance(inst, list) else [inst]
                    assert len(insts) == o.ndma, (o.idx, len(insts), o.ndma)
                    for i in insts:
                        i.then_inc(sems[o.semkey], 16)
                elif o.marked:
                    inst.then_inc(sems[o.semkey], 1)
        return run

    with nc.Block() as block:
        for eng in ("sync", "tensor", "scalar", "vector", "gpsimd"):
            if eng in per_eng:
                getattr(block, eng)(runner(per_eng[eng]))
    es.close()
    ctx.close()
    return nc


_CACHED = {}


def kernel(**inputs):
    if "prog" not in _CACHED:
        _CACHED["prog"] = build_program()
    nc = _CACHED["prog"]
    in_maps = [_host_core_inputs(c, inputs) for c in range(8)]
    res = run_bass_kernel_spmd(nc, in_maps, core_ids=list(range(8)))
    out = np.zeros((B, S, D), np.float32)
    for c in range(8):
        b, half = c // 2, c % 2
        y = res.results[c]["y"]
        for p, g in enumerate(_pos_glist(half)):
            out[b, g * 128:(g + 1) * 128] = y[p * 128:(p + 1) * 128]
    return out
